# version 08
# baseline (speedup 1.0000x reference)
import numpy as np
import ml_dtypes
import concourse.bass as bass
import concourse.mybir as mybir
from concourse.bass_utils import run_bass_kernel_spmd

F32 = mybir.dt.float32
BF16 = mybir.dt.bfloat16
FP16 = mybir.dt.float16
AF = mybir.ActivationFunctionType
OP = mybir.AluOpType
AX = mybir.AxisListType
BFNP = ml_dtypes.bfloat16

L = 4096
DM = 1024
DS = 16
PJ = 4144
FI = 2816
T = 512
CH = L // T
EPS = 1e-5
MH = 1.5 * 2 ** 22
MI = 1.5 * 2 ** 23

ENGS = ("SY", "GP", "VE", "AC", "PE")


class Ctx:
    def __init__(self, sems=None, emit_eng=None, eng=None):
        self.sems = sems
        self.emit_eng = emit_eng
        self.eng = eng
        self.cnt = {e: 0 for e in ENGS}
        self.wm = {e2: 0 for e2 in ENGS}
        self.scan_v = {}
        self.cs_p = {}
        self.yinj_p = {}
        self.tmps_v = {}
        self.pca_q = {}
        self.pca_t = {}

    def op(self, eng, fn, waits=(), inc=0):
        for (e2, v) in waits:
            assert v <= self.cnt[e2], f"wait on future: {eng} waits {e2}>={v} now {self.cnt[e2]}"
        if self.emit_eng == eng:
            for (e2, v) in waits:
                if e2 != eng and v > self.wm[e2]:
                    self.eng.wait_ge(self.sems[e2], v)
                    self.wm[e2] = v
            instr = fn(self.eng)
            if inc:
                instr.then_inc(self.sems[eng], inc)
        self.cnt[eng] += inc

    def val(self, eng):
        return self.cnt[eng]


def build_program(dbg=False, phases='ABWXCD'):
    nc = bass.Bass("TRN2", target_bir_lowering=False)

    di = lambda n, s, d=F32: nc.dram_tensor(n, s, d, kind="ExternalInput").ap()
    u_d = di("u", [L, DM])
    w1_d = di("w1", [PJ, DM])
    cw_d = di("cw", [DM, 1, 4])
    cb_d = di("cb", [DM])
    w2_d = di("w2", [DM, DM + DS])
    lng_d = di("lng", [DM]); lnb_d = di("lnb", [DM])
    gng_d = di("gng", [DM]); gnb_d = di("gnb", [DM])
    alog_d = di("alog", [DM, DS])
    dvec_d = di("dvec", [DM])
    wg_d = di("wg", [FI, DM]); wu_d = di("wu", [FI, DM]); wd_d = di("wd", [DM, FI])
    out_d = nc.dram_tensor("out", [L, DM], F32, kind="ExternalOutput").ap()

    dram = lambda n, s, d=F32: nc.dram_tensor(n, s, d, kind=("ExternalOutput" if dbg else "Internal")).ap()
    y_dram = dram("y_dram", [DM, L], BF16)
    sz_dram = dram("sz_dram", [DM, L], BF16)
    r2_dram = dram("r2_dram", [L, DM], BF16)
    x1_dram = dram("x1_dram", [L, DM], BF16)
    h2qt_dram = dram("h2qt_dram", [FI, L], BF16)
    wdq_dram = dram("wdq_dram", [128, 22 * DM], BF16)
    bc_dram = dram("bc_dram", [CH, 2 * DS * T], BF16)
    gn_dram = dram("gn_dram", [16])
    iden_dram = nc.inline_tensor(np.eye(128, dtype=BFNP), "iden128")

    A = lambda n, s, d=F32: nc.alloc_sbuf_tensor(n, s, d).ap()
    iden = A("iden", [128, 128], BF16)
    acols = A("acols", [128, 128])
    dcol = A("dcol", [128, 8]); cbcol = A("cbcol", [128, 8])
    gngc = A("gngc", [128, 8]); gnbc = A("gnbc", [128, 8])
    cwcol = A("cwcol", [128, 32])
    lngw = A("lngw", [128, DM], BF16); lnbw = A("lnbw", [128, DM], BF16)
    sseq = A("sseq", [16, L], BF16)
    hstate = A("hstate", [128, 128])
    ystats = A("ystats", [128, 128])
    gstat = A("gstat", [128, 16])
    alpha = A("alpha", [128, 8]); beta = A("beta", [128, 8])
    gmw = A("gmw", [128, 8]); rsw = A("rsw", [128, 8])
    ones32 = A("ones32", [128, 1])
    onesb = A("onesb", [128, 16 * 16], BF16)
    small = A("small", [128, 64])
    diag4 = A("diag4", [128, 4 * 128], BF16)
    diagD = A("diagD", [128, 8 * 128], BF16)
    CONSTS = [0.0, 1.0, -1.0, -0.5, EPS, 1e-12, 1.0 / 6.0, 1.0 / 1024, 1.0 / (128.0 * L), 2.0, 0.5]
    cpool = A("cpool", [128, len(CONSTS)])
    for ci, cv in enumerate(CONSTS):
        nc.const_aps.register(F32, cv, cpool[:, ci:ci + 1])

    ARENA_B = 106 * 1024
    WORK_B = 78 * 1024
    ar0, _ = nc.bump_sbuf(ARENA_B)
    wk0, _ = nc.bump_sbuf(WORK_B)

    def at(name, shape, dtype, off):
        return nc.alloc_sbuf_tensor_at(name, shape, dtype, offset=off).ap()

    w1ta = at("w1ta", [128, 8 * 3120], BF16, ar0 + 0)
    w1rt = at("w1rt", [128, 8 * DM], BF16, ar0 + 49920)
    bwall = at("bwall", [128, DS * T], BF16, ar0 + 66304)
    cwall = at("cwall", [128, DS * T], BF16, ar0 + 82688)
    wgq = at("wgq", [128, 8 * FI], BF16, ar0 + 0)
    wuq = at("wuq", [128, 8 * FI], BF16, ar0 + 45056)
    w2t = at("w2t", [128, 9 * DM], BF16, ar0 + 90112)
    wdqt = at("wdqt", [128, 22 * DM], BF16, ar0 + 0)
    assert 108544 <= ARENA_B

    qxn = at("qxn", [128, 1408], F32, wk0 + 0)
    qtf = at("qtf", [128, 1408], F32, wk0 + 5632)
    qbr = at("qbr", [128, 1408], BF16, wk0 + 11264)
    qacc = at("qacc", [128, 1408], BF16, wk0 + 14080)
    mxs = at("mxs", [128, 88], F32, wk0 + 16896)
    mlnt = at("mlnt", [128, 88], F32, wk0 + 17248)
    mrscl = at("mrscl", [128, 88], F32, wk0 + 17600)
    msclb = at("msclb", [128, 88], BF16, wk0 + 17952)
    QEND = 18144
    wraw = at("wraw", [128, FI], F32, wk0 + QEND)
    qnat = at("qnat", [128, FI], BF16, wk0 + QEND + 11264)
    wdstg = at("wdstg", [128, 22 * 128], BF16, wk0 + 0)
    w2stg = at("w2stg", [128, 8 * 1040], BF16, wk0 + QEND)
    PWEND = QEND + 11264 + 5632
    yb = at("yb", [128, 4096], BF16, wk0 + PWEND)
    szb2 = at("szb2", [128, 4096], BF16, wk0 + PWEND + 8192)
    r2rb = at("r2rb", [128, 4096], BF16, wk0 + PWEND + 16384)
    y2f = at("y2f", [128, 4096], BF16, wk0 + PWEND + 24576)
    x1st = at("x1st", [128, 1024], BF16, wk0 + PWEND + 32768)
    x1r = at("x1r", [128, 4096], BF16, wk0 + PWEND)
    xnorm = at("xnorm", [128, 1024], F32, wk0 + PWEND + 8192)
    xq2 = at("xq2", [128, 2 * 1024], BF16, wk0 + PWEND + 12288)
    xqT2 = at("xqT2", [128, 2 * 1024], BF16, wk0 + PWEND + 16384)
    hmid = at("hmid", [128, FI], BF16, wk0 + PWEND + 20480)
    h2q = at("h2q", [128, FI], BF16, wk0 + PWEND + 26112)
    h2qT = at("h2qT", [128, 22 * 128], BF16, wk0 + PWEND + 31744)
    gsil = at("gsil", [128, 512], BF16, wk0 + PWEND + 37376)
    assert PWEND + 38400 <= WORK_B
    h2qtc = at("h2qtc", [128, 22 * T], BF16, wk0 + 0)
    x1r2 = at("x1r2", [128, 4096], BF16, wk0 + 22528)
    outb = at("outb", [128, 1024], F32, wk0 + 30720)
    x_bf = at("x_bf", [128, 4096], BF16, wk0 + 0)
    h_outer = at("h_outer", [128, 4096], BF16, wk0 + 8192)
    h1f = at("h1f", [128, 4096], BF16, wk0 + 16384)
    xbuf = at("xbuf", [128, 8 * (T + 3)], BF16, wk0 + 24576)
    h1stg = at("h1stg", [128, 1024], BF16, wk0 + 32832)
    dtg = at("dtg", [128, 8 * T], FP16, wk0 + 34880)
    xcg = at("xcg", [128, 8 * T], BF16, wk0 + 43072)
    dtxg = at("dtxg", [128, 8 * T], BF16, wk0 + 51264)
    dabuf = at("dabuf", [128, 2 * 4 * T], FP16, wk0 + 0)
    tmps = at("tmps", [128, 2 * 4 * T], BF16, wk0 + 8192)
    ubuf = at("ubuf", [128, 4 * T], BF16, wk0 + 59456)
    hbuf = at("hbuf", [128, 2 * 4 * T], BF16, wk0 + 63552)
    szst = at("szst", [128, T], BF16, wk0 + 72768)
    r2st = at("r2st", [128, T], BF16, wk0 + 73792)
    bcsg = at("bcsg", [48, T], BF16, wk0 + 74816)
    spl = at("spl", [128, 2 * T], FP16, wk0 + 75840)
    sgsig = at("sgsig", [16, T], BF16, wk0 + 77888)
    assert 78912 <= WORK_B
    w1stg = at("w1stg", [128, 16 * DM], BF16, wk0 + 0)
    w1stg2 = at("w1stg2", [48, DM], BF16, wk0 + 32768)

    Ps = lambda n, s, d=F32: nc.alloc_psum_tensor(n, s, d).ap()
    pa = [Ps(f"pa{i}", [128, 512]) for i in range(3)]
    pconv = Ps("pconv", [128, 512])
    pcs = Ps("pcs", [128, 512])
    pb = [Ps(f"pb{i}", [128, 512]) for i in range(2)]
    ptr = nc.alloc_psum_tensor("ptr", [128, 1024], BF16).ap()

    sems = {e: nc.alloc_semaphore(f"sem_{e}") for e in ENGS}

    def walk(ctx: Ctx):
        V = lambda fn, waits=(), inc=0: ctx.op("VE", fn, waits, inc)
        Sc = lambda fn, waits=(), inc=0: ctx.op("AC", fn, waits, inc)
        Pe = lambda fn, waits=(), inc=0: ctx.op("PE", fn, waits, inc)
        Sy = lambda fn, waits=(), inc=0: ctx.op("SY", fn, waits, inc)
        Gp = lambda fn, waits=(), inc=0: ctx.op("GP", fn, waits, inc)
        ptrv = [0]

        for ci, cv in enumerate(CONSTS):
            V(lambda e, ci=ci, cv=cv: e.memset(cpool[:, ci:ci + 1], cv),
              inc=(1 if ci == len(CONSTS) - 1 else 0))
        cpool_v = ctx.val("VE")
        Sy(lambda e: e.dma_start(out=iden[:], in_=iden_dram.ap()), inc=16)
        Sy(lambda e: e.dma_start(out=acols[:].rearrange("p (g s) -> p g s", s=16),
                                 in_=alog_d.rearrange("(g p) s -> p g s", p=128)), inc=16)
        for nm, dn in ((dcol, dvec_d), (cbcol, cb_d), (gngc, gng_d), (gnbc, gnb_d)):
            Sy(lambda e, nm=nm, dn=dn: e.dma_start(out=nm[:], in_=dn.rearrange("(g p) -> p g", p=128),
                                                   allow_slow_non_contiguous=True), inc=16)
        Sy(lambda e: e.dma_start(out=cwcol[:].rearrange("p (g k) -> p g k", k=4),
                                 in_=cw_d.rearrange("(g p) o k -> p g (k o)", p=128)), inc=16)
        Gp(lambda e: e.dma_start(out=lngw[:], in_=lng_d.unsqueeze(0).broadcast_to([128, DM])), inc=16)
        Gp(lambda e: e.dma_start(out=lnbw[:], in_=lnb_d.unsqueeze(0).broadcast_to([128, DM])), inc=16)
        ld0 = ctx.val("SY"); gd0 = ctx.val("GP")
        Sc(lambda e: e.activation(acols[:], acols[:], AF.Exp), waits=[("SY", ld0), ("VE", cpool_v)], inc=1)
        V(lambda e: e.tensor_scalar(acols[:], acols[:], -1.0, None, OP.mult),
          waits=[("AC", ctx.val("AC"))], inc=1)

        prev_v = ctx.val("VE")
        for half in range(2):
            Gp(lambda e, h=half: e.dma_start(
                out=w1stg[:].rearrange("p (a d) -> p a d", a=16),
                in_=w1_d[h * 2048:(h + 1) * 2048, :].rearrange("(a p) d -> p a d", p=128)),
               waits=[("VE", prev_v)], inc=16)
            gready = ctx.val("GP")
            for dk in range(8):
                for grp in range(2):
                    ve_evac = max(ctx.val("VE"), ptrv[0])
                    for jj in range(8):
                        j = grp * 8 + jj
                        Pe(lambda e, dk=dk, j=j, jj=jj: e.matmul(
                            ptr[:, jj * 128:(jj + 1) * 128],
                            w1stg[:, j * 1024 + dk * 128:j * 1024 + dk * 128 + 128],
                            iden[:], start=(jj == 0), stop=(jj == 7), is_transpose=True),
                           waits=[("GP", gready), ("VE", ve_evac), ("SY", ld0)],
                           inc=(1 if jj == 7 else 0))
                    pdone = ctx.val("PE")
                    gj = half * 16 + grp * 8
                    if gj < 24:
                        dst = w1ta[:, dk * 3120 + gj * 128: dk * 3120 + (gj + 8) * 128]
                    else:
                        r = gj - 24
                        dst = w1rt[:, dk * 1024 + r * 128: dk * 1024 + (r + 8) * 128]
                    V(lambda e, dst=dst: e.tensor_copy(dst, ptr[:]), waits=[("PE", pdone)], inc=1)
                    ptrv[0] = ctx.val("VE")
            prev_v = ctx.val("VE")
        Gp(lambda e: e.dma_start(out=w1stg2[:], in_=w1_d[4096:4144, :]),
           waits=[("VE", prev_v)], inc=16)
        gready = ctx.val("GP")
        ve_evac = max(ctx.val("VE"), ptrv[0])
        for dk in range(8):
            Pe(lambda e, dk=dk: e.matmul(
                ptr[:, dk * 128:dk * 128 + 48],
                w1stg2[:, dk * 128:(dk + 1) * 128], iden[0:48, 0:48],
                start=(dk == 0), stop=(dk == 7), is_transpose=True),
               waits=[("GP", gready), ("VE", ve_evac)], inc=(1 if dk == 7 else 0))
        pdone = ctx.val("PE")
        V(lambda e: e.tensor_copy(
            w1ta[:].rearrange("p (dk c) -> p dk c", dk=8)[:, :, 3072:3120],
            ptr[:].rearrange("p (dk c) -> p dk c", dk=8)[:, :, 0:48]),
          waits=[("PE", pdone)], inc=1)
        ptrv[0] = ctx.val("VE")
        V(lambda e: e.memset(hstate[:], 0.0), inc=0)
        V(lambda e: e.memset(xbuf[:], 0.0), inc=0)
        V(lambda e: e.memset(ones32[:], 1.0), inc=0)
        V(lambda e: e.memset(onesb[:], 0.0), inc=1)
        for s in range(16):
            V(lambda e, s=s: e.tensor_copy(onesb[:, s * 16 + s:s * 16 + s + 1], cpool[:, 1:2]),
              inc=(1 if s == 15 else 0))
        for g in range(8):
            V(lambda e, g=g: e.tensor_scalar(
                diagD[:, g * 128:(g + 1) * 128], iden[:], dcol[:, g:g + 1], None, OP.mult),
              inc=(1 if g == 7 else 0))

        acx = {}; acxc = {}; acz = {}; acyev = {}; sttv = {}; vpb0 = {}

        def lnstats(reader, sqbuf, wait_extra):
            V(lambda e: e.tensor_reduce(small[:, 16:17], reader, AX.X, OP.add),
              waits=wait_extra, inc=0)
            V(lambda e: e.tensor_tensor(sqbuf, reader, reader, OP.mult), inc=0)
            V(lambda e: e.tensor_reduce(small[:, 17:18], sqbuf, AX.X, OP.add), inc=1)
            _v = ctx.val("VE")
            Sc(lambda e: e.activation(small[:, 12:13], small[:, 16:17], AF.Copy, scale=1.0 / 1024),
               waits=[("VE", _v)], inc=0)
            Sc(lambda e: e.activation(small[:, 18:19], small[:, 17:18], AF.Copy, scale=1.0 / 1024), inc=1)
            _a = ctx.val("AC")
            V(lambda e: e.scalar_tensor_tensor(small[:, 13:14], small[:, 12:13], small[:, 12:13],
                                               small[:, 18:19], OP.mult, OP.subtract),
              waits=[("AC", _a)], inc=1)
            _v = ctx.val("VE")
            Sc(lambda e: e.activation(small[:, 14:15], small[:, 13:14], AF.Ln, scale=-1.0, bias=EPS),
               waits=[("VE", _v)], inc=0)
            Sc(lambda e: e.activation(small[:, 15:16], small[:, 14:15], AF.Exp, scale=-0.5), inc=1)
            return ctx.val("AC")

        for k in range(CH):
            Gp(lambda e, k=k: e.dma_start(
                out=x_bf[:].rearrange("p (s d) -> p s d", s=4),
                in_=u_d[k * T:(k + 1) * T, :].rearrange("(s p) d -> p s d", p=128)),
               waits=[("VE", ctx.val("VE"))], inc=16)
            xld = ctx.val("GP")
            for sub in range(4):
                xs = x_bf[:, sub * 1024:(sub + 1) * 1024]
                w0 = [("GP", xld)] + ([("PE", ctx.val("PE"))] if (k == 0 and sub == 0) else [])
                sc1 = lnstats(xs, spl[:], w0)
                ho = h_outer[:, sub * 1024:(sub + 1) * 1024]
                V(lambda e, xs=xs, ho=ho: e.tensor_scalar(
                    ho, xs, small[:, 12:13], small[:, 15:16], OP.subtract, OP.mult),
                  waits=[("AC", sc1)], inc=1)
                sc1 = lnstats(ho, spl[:], [])
                V(lambda e, ho=ho: e.tensor_scalar(
                    h1stg[:], ho, small[:, 12:13], small[:, 15:16], OP.subtract, OP.mult),
                  waits=[("AC", sc1)], inc=0)
                V(lambda e: e.tensor_tensor(h1stg[:], h1stg[:], lngw[:], OP.mult),
                  waits=[("GP", gd0)], inc=0)
                V(lambda e: e.tensor_tensor(h1stg[:], h1stg[:], lnbw[:], OP.add), inc=1)
                vh1 = ctx.val("VE")
                for dk in range(8):
                    Pe(lambda e, dk=dk: e.matmul(
                        ptr[:, dk * 128:(dk + 1) * 128],
                        h1stg[:, dk * 128:(dk + 1) * 128], iden[:],
                        start=(dk == 0), stop=(dk == 7), is_transpose=True),
                       waits=[("VE", max(vh1, ptrv[0]))], inc=(1 if dk == 7 else 0))
                pdone = ctx.val("PE")
                V(lambda e, sub=sub: e.tensor_copy(
                    h1f[:].rearrange("p (dk t) -> p dk t", dk=8)[:, :, sub * 128:(sub + 1) * 128],
                    ptr[:].rearrange("p (dk t) -> p dk t", dk=8)),
                  waits=[("PE", pdone)], inc=1)
                ptrv[0] = ctx.val("VE")
            h1f_ready = ctx.val("VE")

            for dk in range(8):
                Pe(lambda e, dk=dk: e.matmul(
                    pa[0][0:48, :], w1ta[:, dk * 3120 + 3072:dk * 3120 + 3120],
                    h1f[:, dk * 512:(dk + 1) * 512],
                    start=(dk == 0), stop=(dk == 7)),
                   waits=[("VE", h1f_ready),
                          ("AC", acx[(k - 1, 7)] if k > 0 else 0)],
                   inc=(1 if dk == 7 else 0))
            pbc = ctx.val("PE")
            V(lambda e: e.tensor_copy(bcsg[:], pa[0][0:48, :]),
              waits=[("PE", pbc), ("SY", ctx.val("SY"))], inc=1)
            vbc = ctx.val("VE")
            Sy(lambda e, k=k: e.dma_start(
                out=bc_dram.rearrange("c (s t) -> c s t", t=T)[k, 0:32, :],
                in_=bcsg[0:32, :]), waits=[("VE", vbc)], inc=16)
            bcout = ctx.val("SY")
            Gp(lambda e: e.dma_start(out=spl[0:16, 0:T], in_=bcsg[32:48, :]),
               waits=[("VE", vbc)], inc=16)
            sgin = ctx.val("GP")
            wall_w = [("SY", bcout)] + ([("VE", ctx.tmps_v[(k - 1) * 32 + 31])] if k > 0 else [])
            Gp(lambda e, k=k: e.dma_start(
                out=bwall[:], in_=bc_dram[k, 0:DS * T].unsqueeze(0).broadcast_to([128, DS * T])),
               waits=wall_w, inc=16)
            Gp(lambda e, k=k: e.dma_start(
                out=cwall[:], in_=bc_dram[k, DS * T:2 * DS * T].unsqueeze(0).broadcast_to([128, DS * T])),
               inc=16)
            walls = ctx.val("GP")
            Sc(lambda e: e.activation(spl[0:16, 0:T], spl[0:16, 0:T], AF.Exp, scale=-1.0),
               waits=[("GP", sgin)], inc=1)
            sge = ctx.val("AC")
            V(lambda e: e.tensor_scalar(spl[0:16, T:2 * T], spl[0:16, 0:T], 1.0, None, OP.add),
              waits=[("AC", sge)], inc=0)
            def _recip(e, k=k):
                with nc.allow_low_precision(reason="sigmoid gate stored bf16"):
                    return e.reciprocal(sgsig[0:16, 0:T], spl[0:16, T:2 * T])
            V(_recip, inc=1)
            v_sgr = ctx.val("VE")

            vstt0 = {0: None, 1: None}
            for g in range(8):
                for kk in range(4):
                    V(lambda e, g=g, kk=kk: e.tensor_scalar(
                        diag4[:, kk * 128:(kk + 1) * 128], iden[:],
                        cwcol[:, g * 4 + kk:g * 4 + kk + 1], None, OP.mult),
                      waits=[("PE", ctx.val("PE")), ("SY", ld0)], inc=(1 if kk == 3 else 0))
                vdiag = ctx.val("VE")
                for dk in range(8):
                    Pe(lambda e, g=g, dk=dk: e.matmul(
                        pa[0][:], w1ta[:, dk * 3120 + g * 128:dk * 3120 + (g + 1) * 128],
                        h1f[:, dk * 512:(dk + 1) * 512],
                        start=(dk == 0), stop=(dk == 7)),
                       waits=[("AC", acx[(k, g - 1)] if g > 0 else (acx[(k - 1, 7)] if k > 0 else 0)),
                              ("VE", vbc)],
                       inc=(1 if dk == 7 else 0))
                pxd = ctx.val("PE")
                Sc(lambda e, g=g: e.activation(
                    xbuf[:, g * (T + 3) + 3:(g + 1) * (T + 3)], pa[0][:], AF.Copy),
                   waits=[("PE", pxd)], inc=1)
                acx[(k, g)] = ctx.val("AC")
                for kk in range(4):
                    Pe(lambda e, g=g, kk=kk: e.matmul(
                        pconv[:], diag4[:, kk * 128:(kk + 1) * 128],
                        xbuf[:, g * (T + 3) + kk:g * (T + 3) + kk + T],
                        start=(kk == 0), stop=(kk == 3)),
                       waits=[("AC", max(acx[(k, g)],
                                         acxc[(k, g - 1)] if g > 0 else (acxc[(k - 1, 7)] if k > 0 else 0))),
                              ("VE", vdiag)],
                       inc=(1 if kk == 3 else 0))
                pcv = ctx.val("PE")
                Sc(lambda e, g=g: e.activation(
                    xcg[:, g * T:(g + 1) * T], pconv[:], AF.Silu, bias=cbcol[:, g:g + 1]),
                   waits=[("PE", pcv)], inc=1)
                acxc[(k, g)] = ctx.val("AC")
                for dk in range(8):
                    Pe(lambda e, g=g, dk=dk: e.matmul(
                        pa[2][:], w1ta[:, dk * 3120 + (8 + g) * 128:dk * 3120 + (9 + g) * 128],
                        h1f[:, dk * 512:(dk + 1) * 512],
                        start=(dk == 0), stop=(dk == 7)),
                       waits=[("AC", acz[(k, g - 1)] if g > 0 else (acz[(k - 1, 7)] if k > 0 else 0))],
                       inc=(1 if dk == 7 else 0))
                pz = ctx.val("PE")
                Sc(lambda e, g=g: e.activation(szst[:], pa[2][:], AF.Silu),
                   waits=[("PE", pz), ("SY", ctx.val("SY"))], inc=1)
                acz[(k, g)] = ctx.val("AC")
                Sy(lambda e, g=g, k=k: e.dma_start(
                    out=sz_dram.rearrange("(g p) t -> p g t", p=128)[:, g, k * T:(k + 1) * T],
                    in_=szst[:]), waits=[("AC", acz[(k, g)])], inc=16)
                if g == 0:
                    Sc(lambda e: e.activation(sgsig[0:16, 0:T], spl[0:16, 0:T], AF.Tanh, scale=0.5),
                       waits=[("GP", sgin)], inc=1)
                    tgh = ctx.val("AC")
                    def _sgf(e):
                        with nc.allow_low_precision(reason="sigmoid gate stored bf16"):
                            return e.tensor_scalar(sgsig[0:16, 0:T], sgsig[0:16, 0:T],
                                                   0.5, 0.5, OP.mult, OP.add)
                    V(_sgf, waits=[("AC", tgh)], inc=1)
                sub, oh = g // 2, g % 2
                ps = pb[oh]
                if vstt0[oh] is None:
                    w_ps = [("AC", acyev[(k - 1, 6 + oh)])] if k > 0 else []
                else:
                    w_ps = [("VE", vstt0[oh])]
                for dk in range(8):
                    Pe(lambda e, sub=sub, oh=oh, dk=dk, ps=ps: e.matmul(
                        ps[:], h1f[:, dk * 512 + sub * 128:dk * 512 + (sub + 1) * 128],
                        w1rt[:, dk * 1024 + oh * 512:dk * 1024 + (oh + 1) * 512],
                        start=(dk == 0), stop=(dk == 7)),
                       waits=w_ps, inc=(1 if dk == 7 else 0))
                pr = ctx.val("PE")
                V(lambda e, sub=sub, oh=oh, ps=ps: e.scalar_tensor_tensor(
                    r2st[:], ps[:], 0.0,
                    h_outer[:, sub * 1024 + oh * 512:sub * 1024 + (oh + 1) * 512],
                    OP.bypass, OP.add),
                  waits=[("PE", pr), ("SY", ctx.val("SY"))], inc=0)
                V(lambda e, sub=sub, oh=oh: e.tensor_tensor(
                    r2st[:], r2st[:],
                    x_bf[:, sub * 1024 + oh * 512:sub * 1024 + (oh + 1) * 512], OP.add), inc=1)
                vstt0[oh] = ctx.val("VE")
                Sy(lambda e, k=k, sub=sub, oh=oh: e.dma_start(
                    out=r2_dram.rearrange("(a p) d -> p a d", p=128)[:, k * 4 + sub, oh * 512:(oh + 1) * 512],
                    in_=r2st[:]), waits=[("VE", vstt0[oh])], inc=16)
            vr2k = ctx.val("VE")

            QT = 4 * T

            def fin_ac(gf):
                Sc(lambda e, gf=gf: e.activation(dtxg[:, gf * T:(gf + 1) * T], pb[gf % 2][:], AF.Copy),
                   waits=[("PE", ctx.yinj_p[k * 32 + gf * 4 + 3])], inc=1)
                acyev[(k, gf)] = ctx.val("AC")

            def fin_vesy(gf):
                yv = dtxg[:, gf * T:(gf + 1) * T]
                V(lambda e, gf=gf, k=k, yv=yv: e.tensor_reduce(
                    ystats[:, gf * CH + k:gf * CH + k + 1], yv, AX.X, OP.add),
                  waits=[("AC", acyev[(k, gf)])], inc=0)
                V(lambda e, yv=yv: e.tensor_tensor(ubuf[:, 0:T], yv, yv, OP.mult), inc=0)
                V(lambda e, gf=gf, k=k: e.tensor_reduce(
                    ystats[:, 64 + gf * CH + k:64 + gf * CH + k + 1],
                    ubuf[:, 0:T], AX.X, OP.add), inc=1)
                vyst = ctx.val("VE")
                Sy(lambda e, gf=gf, k=k, yv=yv: e.dma_start(
                    out=y_dram.rearrange("(g p) t -> p g t", p=128)[:, gf, k * T:(k + 1) * T],
                    in_=yv), waits=[("VE", vyst)], inc=16)

            def softplus(gi, pdt):
                sl2 = (gi % 2) * T
                Sc(lambda e, sl2=sl2: e.activation(spl[:, sl2:sl2 + T], pa[1][:], AF.Abs),
                   waits=[("PE", pdt)], inc=0)
                Sc(lambda e, sl2=sl2: e.activation(spl[:, sl2:sl2 + T], spl[:, sl2:sl2 + T], AF.Exp, scale=-1.0), inc=0)
                Sc(lambda e, sl2=sl2: e.activation(spl[:, sl2:sl2 + T], spl[:, sl2:sl2 + T], AF.Ln, bias=1.0), inc=1)
                return ctx.val("AC")

            acdt = None
            for gi in range(9):
                pdt = None
                if gi < 8:
                    w_dt = ([("VE", sttv[(k, gi - 1)])] if gi > 0 else
                            ([("VE", sttv[(k - 1, 7)])] if k > 0 else []))
                    for dk in range(8):
                        Pe(lambda e, gi=gi, dk=dk: e.matmul(
                            pa[1][:], w1ta[:, dk * 3120 + (16 + gi) * 128:dk * 3120 + (17 + gi) * 128],
                            h1f[:, dk * 512:(dk + 1) * 512],
                            start=(dk == 0), stop=(dk == 7)),
                           waits=(w_dt if dk == 0 else []), inc=(1 if dk == 7 else 0))
                    pdt = ctx.val("PE")
                    if gi == 0:
                        acdt = softplus(0, pdt)
                if gi >= 1:
                    sg = gi - 1
                    if sg >= 2:
                        w_prev = [("AC", acyev[(k, sg - 2)])]
                    elif k > 0:
                        w_prev = [("AC", acyev[(k - 1, 6 + sg)]), ("VE", vstt0[sg % 2])]
                    else:
                        w_prev = [("VE", vstt0[sg % 2])]
                    Pe(lambda e, sg=sg: e.matmul(
                        pb[sg % 2][:], diagD[:, sg * 128:(sg + 1) * 128], xcg[:, sg * T:(sg + 1) * T],
                        start=True, stop=False),
                       waits=[("AC", acxc[(k, sg)])] + w_prev, inc=1)
                    for q in range(4):
                        jq = k * 32 + sg * 4 + q
                        par = (jq % 2) * QT
                        dsl = dabuf[:, par:par + QT]
                        hsl = hbuf[:, par:par + QT]
                        tsl = tmps[:, par:par + QT]
                        w_sc = [("VE", max(ctx.scan_v[jq - 2], vr2k))] if jq >= 2 else [("VE", vr2k)]
                        for m in range(4):
                            j = sg * 16 + q * 4 + m
                            Sc(lambda e, sg=sg, j=j, m=m, d=dsl: e.activation(
                                d[:, m * T:(m + 1) * T], dtg[:, sg * T:(sg + 1) * T],
                                AF.Exp, scale=acols[:, j:j + 1]),
                               waits=(w_sc if m == 0 else []) + [("VE", sttv[(k, sg)])],
                               inc=(1 if m == 3 else 0))
                        acda = ctx.val("AC")
                        if q == 1:
                            if gi < 8:
                                acdt = softplus(gi, pdt)
                            if sg >= 1:
                                fin_ac(sg - 1)
                        V(lambda e, sg=sg, q=q: e.tensor_tensor(
                            ubuf[:].rearrange("p (m t) -> p m t", m=4),
                            dtxg[:, sg * T:(sg + 1) * T].unsqueeze(1).broadcast_to([128, 4, T]),
                            bwall[:, q * QT:(q + 1) * QT].rearrange("p (m t) -> p m t", m=4), OP.mult),
                          waits=[("GP", walls)], inc=0)
                        for m in range(1, 4):
                            j = sg * 16 + q * 4 + m
                            V(lambda e, m=m, j=j, d=dsl: e.scalar_tensor_tensor(
                                ubuf[:, m * T:m * T + 1], d[:, m * T:m * T + 1],
                                hstate[:, j:j + 1], ubuf[:, m * T:m * T + 1], OP.mult, OP.add),
                              waits=[("AC", acda)], inc=0)
                            V(lambda e, m=m, d=dsl: e.memset(d[:, m * T:m * T + 1], 0.0), inc=0)
                        w_h = [("PE", ctx.cs_p[jq - 2])] if jq >= 2 else []
                        V(lambda e, q=q, sg=sg, d=dsl, hs=hsl: e.tensor_tensor_scan(
                            hs, d, ubuf[:], hstate[:, sg * 16 + q * 4:sg * 16 + q * 4 + 1],
                            OP.mult, OP.add),
                          waits=[("AC", acda)] + w_h, inc=1)
                        ctx.scan_v[jq] = ctx.val("VE")
                        w_t = [("PE", ctx.yinj_p[jq - 2])] if jq >= 2 else []
                        V(lambda e, q=q, hs=hsl, ts=tsl: e.tensor_tensor(
                            ts, hs, cwall[:, q * QT:(q + 1) * QT], OP.mult), waits=w_t, inc=1)
                        ctx.tmps_v[jq] = ctx.val("VE")
                        V(lambda e, sg=sg, q=q, hs=hsl: e.tensor_copy(
                            hstate[:, sg * 16 + q * 4:sg * 16 + q * 4 + 4],
                            hs[:].rearrange("p (m t) -> p m t", m=4)[:, :, T - 1:T].rearrange("p m o -> p (m o)")),
                          inc=0)
                        for m in range(4):
                            j = sg * 16 + q * 4 + m
                            Pe(lambda e, j=j, m=m, hs=hsl: e.matmul(
                                pcs[0:16, :], onesb[:, (j % 16) * 16:(j % 16) * 16 + 16],
                                hs[:, m * T:(m + 1) * T],
                                start=(j == 0), stop=(j == 127)),
                               waits=([("VE", ctx.scan_v[jq])] if m == 0 else []), inc=(1 if m == 3 else 0))
                        ctx.cs_p[jq] = ctx.val("PE")
                        for m in range(4):
                            Pe(lambda e, sg=sg, q=q, m=m, ts=tsl: e.matmul(
                                pb[sg % 2][:], iden[:], ts[:, m * T:(m + 1) * T],
                                start=False, stop=(q == 3 and m == 3)),
                               waits=([("VE", ctx.tmps_v[jq])] if m == 0 else []), inc=(1 if m == 3 else 0))
                        ctx.yinj_p[jq] = ctx.val("PE")
                if gi < 8:
                    sl2 = (gi % 2) * T
                    V(lambda e, gi=gi, sl2=sl2: e.scalar_tensor_tensor(
                        dtg[:, gi * T:(gi + 1) * T], pa[1][:], 0.0, spl[:, sl2:sl2 + T], OP.max, OP.add),
                      waits=[("AC", acdt)], inc=1)
                    sttv[(k, gi)] = ctx.val("VE")
                    V(lambda e, gi=gi: e.tensor_tensor(
                        dtxg[:, gi * T:(gi + 1) * T], dtg[:, gi * T:(gi + 1) * T],
                        xcg[:, gi * T:(gi + 1) * T], OP.mult),
                      waits=[("AC", acxc[(k, gi)]), ("SY", ctx.val("SY"))], inc=1)
                if gi >= 2:
                    fin_vesy(gi - 2)
            fin_ac(7)
            fin_vesy(7)

            V(lambda e, k=k: e.scalar_tensor_tensor(
                sseq[0:16, k * T:(k + 1) * T], pcs[0:16, :], 1.0 / DM,
                sgsig[0:16, 0:T], OP.mult, OP.mult),
              waits=[("PE", ctx.cs_p[k * 32 + 31])], inc=1)

            for g in range(8):
                V(lambda e, g=g: e.tensor_copy(
                    xbuf[:, g * (T + 3):g * (T + 3) + 3],
                    xbuf[:, g * (T + 3) + T:g * (T + 3) + T + 3]),
                  waits=[("PE", ctx.val("PE"))], inc=(1 if g == 7 else 0))

        if 'G' not in phases and 'B' not in phases and 'C' not in phases and 'D' not in phases:
            fy = ctx.val('SY')
            Sy(lambda e: e.wait_ge(sems['SY'], fy), inc=0)
            return
        for g in range(8):
            V(lambda e, g=g: e.tensor_reduce(
                small[:, 2 * g:2 * g + 1], ystats[:, g * CH:(g + 1) * CH], AX.X, OP.add), inc=0)
            V(lambda e, g=g: e.tensor_reduce(
                small[:, 2 * g + 1:2 * g + 2], ystats[:, 64 + g * CH:64 + (g + 1) * CH], AX.X, OP.add),
              inc=(1 if g == 7 else 0))
        vgs = ctx.val("VE")
        Pe(lambda e: e.matmul(pcs[0:1, 0:16], ones32[:], small[:, 0:16], start=True, stop=True),
           waits=[("VE", vgs)], inc=1)
        pgs = ctx.val("PE")
        V(lambda e: e.tensor_copy(small[0:1, 16:32], pcs[0:1, 0:16]), waits=[("PE", pgs)], inc=1)
        _v = ctx.val("VE")
        p2 = lambda i: small[0:1, 16:32].rearrange("p (a b) -> p a b", b=2)[:, :, i:i + 1]
        NT = 128.0 * L
        Sc(lambda e: e.activation(small[0:1, 32:40].unsqueeze(2), p2(0), AF.Copy, scale=1.0 / NT),
           waits=[("VE", _v)], inc=0)
        Sc(lambda e: e.activation(small[0:1, 40:48].unsqueeze(2), p2(1), AF.Copy, scale=1.0 / NT), inc=0)
        Sc(lambda e: e.activation(small[0:1, 48:56], small[0:1, 32:40], AF.Square), inc=1)
        _a = ctx.val("AC")
        V(lambda e: e.tensor_tensor(small[0:1, 56:64], small[0:1, 48:56], small[0:1, 40:48], OP.subtract),
          waits=[("AC", _a)], inc=1)
        _v = ctx.val("VE")
        Sc(lambda e: e.activation(small[0:1, 48:56], small[0:1, 56:64], AF.Ln, scale=-1.0, bias=EPS),
           waits=[("VE", _v)], inc=0)
        Sc(lambda e: e.activation(small[0:1, 56:64], small[0:1, 48:56], AF.Exp, scale=-0.5), inc=1)
        acg = ctx.val("AC")
        Sy(lambda e: e.dma_start(out=gn_dram[0:8].unsqueeze(0), in_=small[0:1, 32:40]),
           waits=[("AC", acg)], inc=16)
        Sy(lambda e: e.dma_start(out=gn_dram[8:16].unsqueeze(0), in_=small[0:1, 56:64]), inc=16)
        sgn = ctx.val("SY")
        Gp(lambda e: e.dma_start(out=gmw[:], in_=gn_dram[0:8].unsqueeze(0).broadcast_to([128, 8])),
           waits=[("SY", sgn)], inc=16)
        Gp(lambda e: e.dma_start(out=rsw[:], in_=gn_dram[8:16].unsqueeze(0).broadcast_to([128, 8])), inc=16)
        ggn = ctx.val("GP")
        V(lambda e: e.tensor_tensor(alpha[:], gngc[:], rsw[:], OP.mult), waits=[("GP", ggn)], inc=1)
        _v = ctx.val("VE")
        Sc(lambda e: e.activation(gstat[:, 0:8], alpha[:], AF.Copy), waits=[("VE", _v)], inc=1)
        _a = ctx.val("AC")
        V(lambda e: e.tensor_tensor(small[:, 0:8], gmw[:], gstat[:, 0:8], OP.mult),
          waits=[("AC", _a)], inc=1)
        _v = ctx.val("VE")
        Sc(lambda e: e.activation(gstat[:, 8:16], small[:, 0:8], AF.Copy, scale=-1.0),
           waits=[("VE", _v)], inc=1)
        _a = ctx.val("AC")
        V(lambda e: e.scalar_tensor_tensor(beta[:], gstat[:, 8:16], 1.0, gnbc[:], OP.bypass, OP.add),
          waits=[("AC", _a)], inc=1)
        Gp(lambda e: e.dma_start(out=w2stg[:].rearrange("p (a f) -> p a f", a=8),
                                 in_=w2_d.rearrange("(a p) f -> p a f", p=128)),
           waits=[("PE", ctx.val("PE")), ("VE", ctx.val("VE"))], inc=16)
        gw2 = ctx.val("GP")
        for fk in range(9):
            vprev = max(ctx.val("VE"), ptrv[0])
            for a in range(8):
                if fk < 8:
                    src = w2stg[:, a * 1040 + fk * 128:a * 1040 + (fk + 1) * 128]
                    dstp = ptr[:, a * 128:(a + 1) * 128]
                else:
                    src = w2stg[:, a * 1040 + 1024:a * 1040 + 1040]
                    dstp = ptr[0:16, a * 128:(a + 1) * 128]
                Pe(lambda e, src=src, dstp=dstp, a=a: e.matmul(
                    dstp, src, iden[:], start=(a == 0), stop=(a == 7), is_transpose=True),
                   waits=[("GP", gw2), ("VE", vprev)], inc=(1 if a == 7 else 0))
            pw2 = ctx.val("PE")
            if fk < 8:
                V(lambda e, fk=fk: e.tensor_copy(w2t[:, fk * 1024:(fk + 1) * 1024], ptr[:]),
                  waits=[("PE", pw2)], inc=1)
            else:
                V(lambda e, fk=fk: e.tensor_copy(w2t[0:16, fk * 1024:(fk + 1) * 1024], ptr[0:16, :]),
                  waits=[("PE", pw2)], inc=1)
            ptrv[0] = ctx.val("VE")

        def quant(src, dst, w, waits=()):
            nb = w // 16
            V(lambda e: e.tensor_reduce(
                mxs[:, 0:nb].unsqueeze(2), src.rearrange("p (b x) -> p b x", x=16),
                AX.X, OP.max, apply_absolute_value=True), waits=list(waits), inc=1)
            _v = ctx.val("VE")
            Sc(lambda e: e.activation(mlnt[:, 0:nb], mxs[:, 0:nb], AF.Ln, scale=1.0 / 6.0, bias=1e-12),
               waits=[("VE", _v)], inc=0)
            Sc(lambda e: e.activation(mrscl[:, 0:nb], mlnt[:, 0:nb], AF.Exp, scale=-1.0), inc=0)
            Sc(lambda e: e.activation(msclb[:, 0:nb], mxs[:, 0:nb], AF.Copy, scale=1.0 / 6.0), inc=1)
            _a = ctx.val("AC")
            V(lambda e: e.tensor_tensor(
                qxn[:, 0:w].rearrange("p (b x) -> p b x", x=16),
                src.rearrange("p (b x) -> p b x", x=16),
                mrscl[:, 0:nb].unsqueeze(2).broadcast_to([128, nb, 16]), OP.mult),
              waits=[("AC", _a)], inc=0)
            V(lambda e: e.tensor_scalar(qtf[:, 0:w], qxn[:, 0:w], 2.0, -2.0, OP.min, OP.max), inc=0)
            V(lambda e: e.tensor_scalar(qacc[:, 0:w], qtf[:, 0:w], MH, MH, OP.add, OP.subtract), inc=0)
            V(lambda e: e.tensor_scalar(qtf[:, 0:w], qxn[:, 0:w], 2.0, 4.0, OP.max, OP.min), inc=0)
            V(lambda e: e.tensor_scalar(qbr[:, 0:w], qtf[:, 0:w], MI, MI + 2.0, OP.add, OP.subtract), inc=0)
            V(lambda e: e.tensor_tensor(qacc[:, 0:w], qacc[:, 0:w], qbr[:, 0:w], OP.add), inc=0)
            V(lambda e: e.tensor_scalar(qtf[:, 0:w], qxn[:, 0:w], -2.0, -4.0, OP.min, OP.max), inc=0)
            V(lambda e: e.tensor_scalar(qbr[:, 0:w], qtf[:, 0:w], MI, MI - 2.0, OP.add, OP.subtract), inc=0)
            V(lambda e: e.tensor_tensor(qacc[:, 0:w], qacc[:, 0:w], qbr[:, 0:w], OP.add), inc=0)
            V(lambda e: e.tensor_scalar(qbr[:, 0:w], qxn[:, 0:w], 5.0, 2.0, OP.is_gt, OP.mult), inc=0)
            V(lambda e: e.tensor_tensor(qacc[:, 0:w], qacc[:, 0:w], qbr[:, 0:w], OP.add), inc=0)
            V(lambda e: e.tensor_scalar(qbr[:, 0:w], qxn[:, 0:w], -5.0, -2.0, OP.is_lt, OP.mult), inc=0)
            V(lambda e: e.tensor_tensor(qacc[:, 0:w], qacc[:, 0:w], qbr[:, 0:w], OP.add), inc=0)
            V(lambda e: e.tensor_tensor(
                dst.rearrange("p (b x) -> p b x", x=16),
                qacc[:, 0:w].rearrange("p (b x) -> p b x", x=16),
                msclb[:, 0:nb].unsqueeze(2).broadcast_to([128, nb, 16]), OP.mult), inc=1)

        if 'B' not in phases:
            fy = ctx.val('SY')
            Sy(lambda e: e.wait_ge(sems['SY'], fy), inc=0)
            return
        pwjobs = (([("g", i) for i in range(22)] + [("u", i) for i in range(22)]) if 'W' in phases else []) \
                 + ([("d", i) for i in range(8)] if 'X' in phases else [])
        for kb in range(CH):
            vdone = ctx.val("VE")
            Sy(lambda e, kb=kb: e.dma_start(
                out=yb[:].rearrange("p (g t) -> p g t", g=8),
                in_=y_dram.rearrange("(g p) t -> p g t", p=128)[:, :, kb * T:(kb + 1) * T]),
               waits=[("VE", vdone)], inc=16)
            Sy(lambda e, kb=kb: e.dma_start(
                out=szb2[:].rearrange("p (g t) -> p g t", g=8),
                in_=sz_dram.rearrange("(g p) t -> p g t", p=128)[:, :, kb * T:(kb + 1) * T]), inc=16)
            Sy(lambda e, kb=kb: e.dma_start(
                out=r2rb[:].rearrange("p (a d) -> p a d", a=4),
                in_=r2_dram[kb * T:(kb + 1) * T, :].rearrange("(a p) d -> p a d", p=128)), inc=16)
            syld = ctx.val("SY")
            for g in range(8):
                V(lambda e, g=g: e.tensor_scalar(
                    y2f[:, g * T:(g + 1) * T], yb[:, g * T:(g + 1) * T],
                    alpha[:, g:g + 1], beta[:, g:g + 1], OP.mult, OP.add),
                  waits=[("SY", syld)], inc=0)
                V(lambda e, g=g: e.tensor_tensor(
                    y2f[:, g * T:(g + 1) * T], y2f[:, g * T:(g + 1) * T],
                    szb2[:, g * T:(g + 1) * T], OP.mult), inc=(1 if g == 7 else 0))
            vy2 = ctx.val("VE")
            vstt = {0: None, 1: None}
            for sub in range(4):
                for oh in range(2):
                    ps = pb[oh]
                    vprev = vstt[oh] if vstt[oh] is not None else ctx.val("VE")
                    for fk in range(9):
                        if fk < 8:
                            stat = y2f[:, fk * T + sub * 128:fk * T + (sub + 1) * 128]
                            mov = w2t[:, fk * 1024 + oh * 512:fk * 1024 + (oh + 1) * 512]
                        else:
                            stat = sseq[0:16, kb * T + sub * 128:kb * T + (sub + 1) * 128]
                            mov = w2t[0:16, 8 * 1024 + oh * 512:8 * 1024 + (oh + 1) * 512]
                        Pe(lambda e, stat=stat, mov=mov, fk=fk, ps=ps: e.matmul(
                            ps[:], stat, mov, start=(fk == 0), stop=(fk == 8)),
                           waits=[("VE", max(vy2, vprev))], inc=(1 if fk == 8 else 0))
                    po = ctx.val("PE")
                    V(lambda e, sub=sub, oh=oh, ps=ps: e.scalar_tensor_tensor(
                        x1st[:, oh * T:(oh + 1) * T], ps[:], 0.0,
                        r2rb[:, sub * 1024 + oh * 512:sub * 1024 + (oh + 1) * 512],
                        OP.bypass, OP.add),
                      waits=[("PE", po), ("SY", ctx.val("SY"))], inc=1)
                    vstt[oh] = ctx.val("VE")
                    vx1 = ctx.val("VE")
                    Sy(lambda e, kb=kb, sub=sub, oh=oh: e.dma_start(
                        out=x1_dram.rearrange("(a p) d -> p a d", p=128)[:, kb * 4 + sub, oh * 512:(oh + 1) * 512],
                        in_=x1st[:, oh * T:(oh + 1) * T]), waits=[("VE", vx1)], inc=16)
            for (wt, i) in (pwjobs[kb * len(pwjobs) // CH:(kb + 1) * len(pwjobs) // CH] if pwjobs else []):
                if wt in ("g", "u"):
                    wsrc = wg_d if wt == "g" else wu_d
                    wdst = wgq if wt == "g" else wuq
                    vprev = ctx.val("VE")
                    Sy(lambda e, wsrc=wsrc, i=i: e.dma_start(
                        out=wraw[:, 0:1024], in_=wsrc[i * 128:(i + 1) * 128, :]),
                       waits=[("VE", vprev)], inc=16)
                    sld = ctx.val("SY")
                    quant(wraw[:, 0:1024], qnat[:, 0:1024], 1024, waits=[("SY", sld)])
                    vq = max(ctx.val("VE"), ptrv[0])
                    for dk in range(8):
                        Pe(lambda e, dk=dk: e.matmul(
                            ptr[:, dk * 128:(dk + 1) * 128], qnat[:, dk * 128:(dk + 1) * 128],
                            iden[:], start=(dk == 0), stop=(dk == 7), is_transpose=True),
                           waits=[("VE", vq)], inc=(1 if dk == 7 else 0))
                    pq = ctx.val("PE")
                    V(lambda e, wdst=wdst, i=i: e.tensor_copy(
                        wdst.rearrange("p (dk c) -> p dk c", dk=8)[:, :, i * 128:(i + 1) * 128],
                        ptr[:].rearrange("p (dk c) -> p dk c", dk=8)),
                      waits=[("PE", pq)], inc=1)
                    ptrv[0] = ctx.val("VE")
                else:
                    vprev = ctx.val("VE")
                    Sy(lambda e, i=i: e.dma_start(out=wraw[:], in_=wd_d[i * 128:(i + 1) * 128, :]),
                       waits=[("VE", vprev)], inc=16)
                    sld = ctx.val("SY")
                    quant(wraw[:, 0:1408], qnat[:, 0:1408], 1408, waits=[("SY", sld)])
                    quant(wraw[:, 1408:2816], qnat[:, 1408:2816], 1408)
                    vq = max(ctx.val("VE"), ptrv[0])
                    for b0 in range(0, 22, 8):
                        nbt = min(8, 22 - b0)
                        vpr = ctx.val("VE")
                        for t in range(nbt):
                            nk = b0 + t
                            Pe(lambda e, nk=nk, t=t, nbt=nbt: e.matmul(
                                ptr[:, t * 128:(t + 1) * 128], qnat[:, nk * 128:(nk + 1) * 128],
                                iden[:], start=(t == 0), stop=(t == nbt - 1), is_transpose=True),
                               waits=[("VE", max(vq, vpr))], inc=(1 if t == nbt - 1 else 0))
                        pq = ctx.val("PE")
                        V(lambda e, b0=b0, nbt=nbt: e.tensor_copy(
                            wdstg[:, b0 * 128:(b0 + nbt) * 128], ptr[:, 0:nbt * 128]),
                          waits=[("PE", pq), ("SY", ctx.val("SY"))], inc=1)
                        ptrv[0] = ctx.val("VE")
                    vqd = ctx.val("VE")
                    Sy(lambda e, i=i: e.dma_start(
                        out=wdq_dram.rearrange("p (nk c) -> p nk c", nk=22)[:, :, i * 128:(i + 1) * 128],
                        in_=wdstg[:].rearrange("p (nk c) -> p nk c", nk=22)),
                       waits=[("VE", vqd)], inc=16)

        if 'C' not in phases:
            fy = ctx.val('SY')
            Sy(lambda e: e.wait_ge(sems['SY'], fy), inc=0)
            return
        for kc in range(CH):
            vdone = ctx.val("VE")
            Sy(lambda e, kc=kc: e.dma_start(
                out=x1r[:].rearrange("p (a d) -> p a d", a=4),
                in_=x1_dram[kc * T:(kc + 1) * T, :].rearrange("(a p) d -> p a d", p=128)),
               waits=[("VE", vdone)], inc=16)
            sx1 = ctx.val("SY")
            for it in range(5):
                if it < 4:
                    sub = it
                    par = sub % 2
                    xs = x1r[:, sub * 1024:(sub + 1) * 1024]
                    V(lambda e, xs=xs: e.tensor_reduce(small[:, 16:17], xs, AX.X, OP.add),
                      waits=[("SY", sx1)], inc=0)
                    V(lambda e, xs=xs: e.tensor_tensor(qtf[:, 0:1024], xs, xs, OP.mult), inc=0)
                    V(lambda e: e.tensor_reduce(small[:, 17:18], qtf[:, 0:1024], AX.X, OP.add), inc=1)
                    _v = ctx.val("VE")
                    Sc(lambda e: e.activation(small[:, 12:13], small[:, 16:17], AF.Copy, scale=1.0 / 1024),
                       waits=[("VE", _v)], inc=0)
                    Sc(lambda e: e.activation(small[:, 18:19], small[:, 17:18], AF.Copy, scale=1.0 / 1024), inc=1)
                    _a = ctx.val("AC")
                    V(lambda e: e.scalar_tensor_tensor(small[:, 13:14], small[:, 12:13], small[:, 12:13],
                                                       small[:, 18:19], OP.mult, OP.subtract),
                      waits=[("AC", _a)], inc=1)
                    _v = ctx.val("VE")
                    Sc(lambda e: e.activation(small[:, 14:15], small[:, 13:14], AF.Ln, scale=-1.0, bias=EPS),
                       waits=[("VE", _v)], inc=0)
                    Sc(lambda e: e.activation(small[:, 15:16], small[:, 14:15], AF.Exp, scale=-0.5), inc=1)
                    sc1 = ctx.val("AC")
                    V(lambda e, xs=xs: e.tensor_scalar(
                        xnorm[:], xs, small[:, 12:13], small[:, 15:16], OP.subtract, OP.mult),
                      waits=[("AC", sc1)], inc=1)
                    quant(xnorm[:], xq2[:, par * 1024:(par + 1) * 1024], 1024)
                    ctx.pca_q[(kc, sub)] = ctx.val("VE")
                if it >= 1:
                    sub = it - 1
                    par = sub % 2
                    xqt = xqT2[:, par * 1024:(par + 1) * 1024]
                    vxqt = ctx.pca_t[(kc, sub)]
                    ac_sil = {}
                    ve_hm = {}
                    for nb in range(6):
                        noff = nb * 512
                        nw = min(512, FI - noff)
                        pg_ = [pa[0], pa[2]][nb % 2]
                        pu_ = [pa[1], pconv][nb % 2]
                        w_g = [("VE", vxqt), ("AC", ac_sil.get(nb - 2, ctx.val("AC") if nb < 2 else 0))]
                        if nb < 2:
                            w_g = [("VE", vxqt), ("AC", ctx.val("AC"))]
                        else:
                            w_g = [("VE", vxqt), ("AC", ac_sil[nb - 2])]
                        for dk in range(8):
                            Pe(lambda e, dk=dk, noff=noff, nw=nw, xqt=xqt, pg_=pg_: e.matmul(
                                pg_[:, 0:nw], xqt[:, dk * 128:(dk + 1) * 128],
                                wgq[:, dk * FI + noff:dk * FI + noff + nw],
                                start=(dk == 0), stop=(dk == 7)),
                               waits=w_g, inc=(1 if dk == 7 else 0))
                        w_u = [("VE", ve_hm[nb - 2])] if nb >= 2 else [("VE", vxqt)]
                        for dk in range(8):
                            Pe(lambda e, dk=dk, noff=noff, nw=nw, xqt=xqt, pu_=pu_: e.matmul(
                                pu_[:, 0:nw], xqt[:, dk * 128:(dk + 1) * 128],
                                wuq[:, dk * FI + noff:dk * FI + noff + nw],
                                start=(dk == 0), stop=(dk == 7)),
                               waits=w_u, inc=(1 if dk == 7 else 0))
                        pgu = ctx.val("PE")
                        Sc(lambda e, nw=nw, pg_=pg_: e.activation(gsil[:, 0:nw], pg_[:, 0:nw], AF.Silu),
                           waits=[("PE", pgu)], inc=1)
                        ac_sil[nb] = ctx.val("AC")
                        V(lambda e, noff=noff, nw=nw, pu_=pu_: e.tensor_tensor(
                            hmid[:, noff:noff + nw], gsil[:, 0:nw], pu_[:, 0:nw], OP.mult),
                          waits=[("AC", ac_sil[nb])], inc=1)
                        ve_hm[nb] = ctx.val("VE")
                    quant(hmid[:, 0:1408], h2q[:, 0:1408], 1408)
                    quant(hmid[:, 1408:2816], h2q[:, 1408:2816], 1408)
                    vq2 = ctx.val("VE")
                    for b0 in range(0, 22, 8):
                        nbt = min(8, 22 - b0)
                        vpr = max(ctx.val("VE"), ptrv[0])
                        for t in range(nbt):
                            nk = b0 + t
                            Pe(lambda e, nk=nk, t=t, nbt=nbt: e.matmul(
                                ptr[:, t * 128:(t + 1) * 128], h2q[:, nk * 128:(nk + 1) * 128],
                                iden[:], start=(t == 0), stop=(t == nbt - 1), is_transpose=True),
                               waits=[("VE", max(vq2, vpr))], inc=(1 if t == nbt - 1 else 0))
                        ph = ctx.val("PE")
                        V(lambda e, b0=b0, nbt=nbt: e.tensor_copy(
                            h2qT[:, b0 * 128:(b0 + nbt) * 128], ptr[:, 0:nbt * 128]),
                          waits=[("PE", ph), ("SY", ctx.val("SY"))], inc=1)
                        ptrv[0] = ctx.val("VE")
                    vsp = ctx.val("VE")
                    Sy(lambda e, kc=kc, sub=sub: e.dma_start(
                        out=h2qt_dram.rearrange("(nk p) t -> p nk t", p=128)[:, :, kc * T + sub * 128:kc * T + (sub + 1) * 128],
                        in_=h2qT[:].rearrange("p (nk c) -> p nk c", nk=22)),
                       waits=[("VE", vsp)], inc=16)
                if it < 4:
                    sub = it
                    par = sub % 2
                    vq = max(ctx.pca_q[(kc, sub)], ptrv[0])
                    for dk in range(8):
                        Pe(lambda e, dk=dk, par=par: e.matmul(
                            ptr[:, dk * 128:(dk + 1) * 128],
                            xq2[:, par * 1024 + dk * 128:par * 1024 + (dk + 1) * 128],
                            iden[:], start=(dk == 0), stop=(dk == 7), is_transpose=True),
                           waits=[("VE", vq)], inc=(1 if dk == 7 else 0))
                    pxq = ctx.val("PE")
                    V(lambda e, par=par: e.tensor_copy(xqT2[:, par * 1024:(par + 1) * 1024], ptr[:]),
                      waits=[("PE", pxq)], inc=1)
                    ptrv[0] = ctx.val("VE")
                    ctx.pca_t[(kc, sub)] = ctx.val("VE")

        if 'D' not in phases:
            fy = ctx.val('SY')
            Sy(lambda e: e.wait_ge(sems['SY'], fy), inc=0)
            return
        vdone = ctx.val("VE")
        Sy(lambda e: e.dma_start(out=wdqt[:], in_=wdq_dram[:, :]),
           waits=[("VE", vdone), ("PE", ctx.val("PE"))], inc=16)
        swd = ctx.val("SY")
        for kd in range(CH):
            vdone = ctx.val("VE")
            Sy(lambda e, kd=kd: e.dma_start(
                out=h2qtc[:].rearrange("p (nk t) -> p nk t", nk=22),
                in_=h2qt_dram.rearrange("(nk p) t -> p nk t", p=128)[:, :, kd * T:(kd + 1) * T]),
               waits=[("VE", vdone)], inc=16)
            Sy(lambda e, kd=kd: e.dma_start(
                out=x1r2[:].rearrange("p (a d) -> p a d", a=4),
                in_=x1_dram[kd * T:(kd + 1) * T, :].rearrange("(a p) d -> p a d", p=128)), inc=16)
            sld = ctx.val("SY")
            vstt = {0: None, 1: None}
            for sub in range(4):
                for oh in range(2):
                    ps = pb[oh]
                    vprev = vstt[oh] if vstt[oh] is not None else ctx.val("VE")
                    for nk in range(22):
                        Pe(lambda e, sub=sub, oh=oh, nk=nk, ps=ps: e.matmul(
                            ps[:], h2qtc[:, nk * T + sub * 128:nk * T + (sub + 1) * 128],
                            wdqt[:, nk * 1024 + oh * 512:nk * 1024 + (oh + 1) * 512],
                            start=(nk == 0), stop=(nk == 21)),
                           waits=[("VE", vprev), ("SY", max(sld, swd))], inc=(1 if nk == 21 else 0))
                    pd = ctx.val("PE")
                    V(lambda e, sub=sub, oh=oh, ps=ps: e.scalar_tensor_tensor(
                        outb[:, oh * T:(oh + 1) * T], ps[:], 0.0,
                        x1r2[:, sub * 1024 + oh * 512:sub * 1024 + (oh + 1) * 512],
                        OP.bypass, OP.add),
                      waits=[("PE", pd), ("SY", ctx.val("SY"))], inc=1)
                    vstt[oh] = ctx.val("VE")
                    vo = ctx.val("VE")
                    Sy(lambda e, kd=kd, sub=sub, oh=oh: e.dma_start(
                        out=out_d.rearrange("(a p) d -> p a d", p=128)[:, kd * 4 + sub, oh * 512:(oh + 1) * 512],
                        in_=outb[:, oh * T:(oh + 1) * T]), waits=[("VE", vo)], inc=16)
        final_sy = ctx.val("SY")
        Sy(lambda e: e.wait_ge(sems["SY"], final_sy), inc=0)

    ctx0 = Ctx()
    walk(ctx0)

    with nc.Block() as block:
        def emit(eng_name):
            def inner(e):
                ctx = Ctx(sems, eng_name, e)
                walk(ctx)
            return inner
        block.sync(emit("SY"))
        block.gpsimd(emit("GP"))
        block.vector(emit("VE"))
        block.scalar(emit("AC"))
        block.tensor(emit("PE"))
    return nc


_NC_CACHE = None


def kernel(**inputs):
    global _NC_CACHE
    u = np.ascontiguousarray(np.asarray(inputs["L0_attn_u"], dtype=np.float32))
    base = {
        "w1": np.ascontiguousarray(np.asarray(inputs["L0_attn_in_proj_w"], dtype=np.float32)),
        "cw": np.ascontiguousarray(np.asarray(inputs["L0_attn_conv_w"], dtype=np.float32)),
        "cb": np.ascontiguousarray(np.asarray(inputs["L0_attn_conv_b"], dtype=np.float32)),
        "w2": np.ascontiguousarray(np.asarray(inputs["L0_attn_out_proj_w"], dtype=np.float32)),
        "lng": np.ascontiguousarray(np.asarray(inputs["L0_attn_ln_g"], dtype=np.float32)),
        "lnb": np.ascontiguousarray(np.asarray(inputs["L0_attn_ln_b"], dtype=np.float32)),
        "gng": np.ascontiguousarray(np.asarray(inputs["L0_attn_gn_g"], dtype=np.float32)),
        "gnb": np.ascontiguousarray(np.asarray(inputs["L0_attn_gn_b"], dtype=np.float32)),
        "alog": np.ascontiguousarray(np.asarray(inputs["L0_attn_A_log"], dtype=np.float32)),
        "dvec": np.ascontiguousarray(np.asarray(inputs["L0_attn_D"], dtype=np.float32)),
        "wg": np.ascontiguousarray(np.asarray(inputs["L0_ffn_w_gate"], dtype=np.float32)),
        "wu": np.ascontiguousarray(np.asarray(inputs["L0_ffn_w_up"], dtype=np.float32)),
        "wd": np.ascontiguousarray(np.asarray(inputs["L0_ffn_w_down"], dtype=np.float32)),
    }
    if _NC_CACHE is None:
        _NC_CACHE = build_program()
    nc = _NC_CACHE
    in_maps = [dict(base, u=np.ascontiguousarray(u[i])) for i in range(8)]
    res = run_bass_kernel_spmd(nc, in_maps, core_ids=list(range(8)))
    out = np.stack([np.asarray(res.results[i]["out"]) for i in range(8)], axis=0)
    return out.astype(np.float32)


if __name__ == "__main__":
    build_program()
    print("built ok")


# version 09 (final)
# speedup vs baseline: 1.1777x; 1.1777x over previous
import numpy as np
import ml_dtypes
import concourse.bass as bass
import concourse.mybir as mybir
from concourse.bass_utils import run_bass_kernel_spmd

F32 = mybir.dt.float32
BF16 = mybir.dt.bfloat16
FP16 = mybir.dt.float16
AF = mybir.ActivationFunctionType
OP = mybir.AluOpType
AX = mybir.AxisListType
BFNP = ml_dtypes.bfloat16

L = 4096
DM = 1024
DS = 16
PJ = 4144
FI = 2816
T = 512
CH = L // T
EPS = 1e-5
MH = 1.5 * 2 ** 22
MI = 1.5 * 2 ** 23

ENGS = ("SY", "GP", "VE", "AC", "PE")


class Ctx:
    def __init__(self, sems=None, emit_eng=None, eng=None):
        self.sems = sems
        self.emit_eng = emit_eng
        self.eng = eng
        self.cnt = {e: 0 for e in ENGS}
        self.wm = {e2: 0 for e2 in ENGS}
        self.scan_v = {}
        self.cs_p = {}
        self.yinj_p = {}
        self.tmps_v = {}
        self.pca_q = {}
        self.pca_t = {}

    def op(self, eng, fn, waits=(), inc=0):
        for (e2, v) in waits:
            assert v <= self.cnt[e2], f"wait on future: {eng} waits {e2}>={v} now {self.cnt[e2]}"
        if self.emit_eng == eng:
            for (e2, v) in waits:
                if e2 != eng and v > self.wm[e2]:
                    self.eng.wait_ge(self.sems[e2], v)
                    self.wm[e2] = v
            instr = fn(self.eng)
            if inc:
                instr.then_inc(self.sems[eng], inc)
        self.cnt[eng] += inc

    def val(self, eng):
        return self.cnt[eng]


def build_program(dbg=False, phases='ABWXCD'):
    nc = bass.Bass("TRN2", target_bir_lowering=False)

    di = lambda n, s, d=F32: nc.dram_tensor(n, s, d, kind="ExternalInput").ap()
    u_d = di("u", [L, DM])
    w1_d = di("w1", [PJ, DM])
    cw_d = di("cw", [DM, 1, 4])
    cb_d = di("cb", [DM])
    w2_d = di("w2", [DM, DM + DS])
    lng_d = di("lng", [DM]); lnb_d = di("lnb", [DM])
    gng_d = di("gng", [DM]); gnb_d = di("gnb", [DM])
    alog_d = di("alog", [DM, DS])
    dvec_d = di("dvec", [DM])
    wg_d = di("wg", [FI, DM]); wu_d = di("wu", [FI, DM]); wd_d = di("wd", [DM, FI])
    out_d = nc.dram_tensor("out", [L, DM], F32, kind="ExternalOutput").ap()

    dram = lambda n, s, d=F32: nc.dram_tensor(n, s, d, kind=("ExternalOutput" if dbg else "Internal")).ap()
    y_dram = dram("y_dram", [DM, L], BF16)
    sz_dram = dram("sz_dram", [DM, L], BF16)
    r2_dram = dram("r2_dram", [L, DM], BF16)
    x1_dram = dram("x1_dram", [L, DM], BF16)
    h2qt_dram = dram("h2qt_dram", [FI, L], BF16)
    wdq_dram = dram("wdq_dram", [128, 22 * DM], BF16)
    bc_dram = dram("bc_dram", [CH, 2 * DS * T], BF16)
    gn_dram = dram("gn_dram", [16])
    iden_dram = nc.inline_tensor(np.eye(128, dtype=BFNP), "iden128")

    A = lambda n, s, d=F32: nc.alloc_sbuf_tensor(n, s, d).ap()
    iden = A("iden", [128, 128], BF16)
    acols = A("acols", [128, 128])
    dcol = A("dcol", [128, 8]); cbcol = A("cbcol", [128, 8])
    gngc = A("gngc", [128, 8]); gnbc = A("gnbc", [128, 8])
    cwcol = A("cwcol", [128, 32])
    lngw = A("lngw", [128, DM], BF16); lnbw = A("lnbw", [128, DM], BF16)
    sseq = A("sseq", [16, L], BF16)
    hstate = A("hstate", [128, 128])
    ystats = A("ystats", [128, 128])
    gstat = A("gstat", [128, 16])
    alpha = A("alpha", [128, 8]); beta = A("beta", [128, 8])
    gmw = A("gmw", [128, 8]); rsw = A("rsw", [128, 8])
    ones32 = A("ones32", [128, 1])
    onesb = A("onesb", [128, 16 * 16], BF16)
    small = A("small", [128, 64])
    diag4 = A("diag4", [128, 4 * 128], BF16)
    diagD = A("diagD", [128, 8 * 128], BF16)
    CONSTS = [0.0, 1.0, -1.0, -0.5, EPS, 1e-12, 1.0 / 6.0, 1.0 / 1024, 1.0 / (128.0 * L), 2.0, 0.5]
    cpool = A("cpool", [128, len(CONSTS)])
    for ci, cv in enumerate(CONSTS):
        nc.const_aps.register(F32, cv, cpool[:, ci:ci + 1])

    ARENA_B = 106 * 1024
    WORK_B = 78 * 1024
    ar0, _ = nc.bump_sbuf(ARENA_B)
    wk0, _ = nc.bump_sbuf(WORK_B)

    def at(name, shape, dtype, off):
        return nc.alloc_sbuf_tensor_at(name, shape, dtype, offset=off).ap()

    w1ta = at("w1ta", [128, 8 * 3120], BF16, ar0 + 0)
    w1rt = at("w1rt", [128, 8 * DM], BF16, ar0 + 49920)
    bwall = at("bwall", [128, DS * T], BF16, ar0 + 66304)
    cwall = at("cwall", [128, DS * T], BF16, ar0 + 82688)
    wgq = at("wgq", [128, 8 * FI], BF16, ar0 + 0)
    wuq = at("wuq", [128, 8 * FI], BF16, ar0 + 45056)
    w2t = at("w2t", [128, 9 * DM], BF16, ar0 + 90112)
    wdqt = at("wdqt", [128, 22 * DM], BF16, ar0 + 0)
    assert 108544 <= ARENA_B

    qxn = at("qxn", [128, 1408], F32, wk0 + 0)
    qtf = at("qtf", [128, 1408], F32, wk0 + 5632)
    qbr = at("qbr", [128, 1408], BF16, wk0 + 11264)
    qacc = at("qacc", [128, 1408], BF16, wk0 + 14080)
    mxs = at("mxs", [128, 88], F32, wk0 + 16896)
    mlnt = at("mlnt", [128, 88], F32, wk0 + 17248)
    mrscl = at("mrscl", [128, 88], F32, wk0 + 17600)
    msclb = at("msclb", [128, 88], BF16, wk0 + 17952)
    QEND = 18144
    wraw = at("wraw", [128, FI], F32, wk0 + QEND)
    qnat = at("qnat", [128, FI], BF16, wk0 + QEND + 11264)
    wdstg = at("wdstg", [128, 22 * 128], BF16, wk0 + 0)
    w2stg = at("w2stg", [128, 8 * 1040], BF16, wk0 + QEND)
    PWEND = QEND + 11264 + 5632
    yb = at("yb", [128, 4096], BF16, wk0 + PWEND)
    szb2 = at("szb2", [128, 4096], BF16, wk0 + PWEND + 8192)
    r2rb = at("r2rb", [128, 4096], BF16, wk0 + PWEND + 16384)
    y2f = at("y2f", [128, 4096], BF16, wk0 + PWEND + 24576)
    x1st = at("x1st", [128, 1024], BF16, wk0 + PWEND + 32768)
    x1r = at("x1r", [128, 4096], BF16, wk0 + PWEND)
    xnorm = at("xnorm", [128, 1024], F32, wk0 + PWEND + 8192)
    xq2 = at("xq2", [128, 2 * 1024], BF16, wk0 + PWEND + 12288)
    xqT2 = at("xqT2", [128, 2 * 1024], BF16, wk0 + PWEND + 16384)
    hmid = at("hmid", [128, FI], BF16, wk0 + PWEND + 20480)
    h2q = at("h2q", [128, FI], BF16, wk0 + PWEND + 26112)
    h2qT = at("h2qT", [128, 22 * 128], BF16, wk0 + PWEND + 31744)
    gsil = at("gsil", [128, 512], BF16, wk0 + PWEND + 37376)
    assert PWEND + 38400 <= WORK_B
    h2qtc = at("h2qtc", [128, 22 * T], BF16, wk0 + 0)
    x1r2 = at("x1r2", [128, 4096], BF16, wk0 + 22528)
    outb = at("outb", [128, 1024], F32, wk0 + 30720)
    x_bf = at("x_bf", [128, 4096], BF16, wk0 + 0)
    h_outer = at("h_outer", [128, 4096], BF16, wk0 + 8192)
    h1f = at("h1f", [128, 4096], BF16, wk0 + 16384)
    xbuf = at("xbuf", [128, 8 * (T + 3)], BF16, wk0 + 24576)
    h1stg = at("h1stg", [128, 1024], BF16, wk0 + 32832)
    dtg = at("dtg", [128, 8 * T], FP16, wk0 + 34880)
    xcg = at("xcg", [128, 8 * T], BF16, wk0 + 43072)
    dtxg = at("dtxg", [128, 8 * T], BF16, wk0 + 51264)
    dabuf = at("dabuf", [128, 2 * 4 * T], FP16, wk0 + 0)
    tmps = at("tmps", [128, 2 * 4 * T], BF16, wk0 + 8192)
    ubuf = at("ubuf", [128, 4 * T], BF16, wk0 + 59456)
    hbuf = at("hbuf", [128, 2 * 4 * T], BF16, wk0 + 63552)
    szst = at("szst", [128, T], BF16, wk0 + 72768)
    r2st = at("r2st", [128, T], BF16, wk0 + 73792)
    bcsg = at("bcsg", [48, T], BF16, wk0 + 74816)
    spl = at("spl", [128, 2 * T], FP16, wk0 + 75840)
    sgsig = at("sgsig", [16, T], BF16, wk0 + 77888)
    assert 78912 <= WORK_B
    w1stg = at("w1stg", [128, 16 * DM], BF16, wk0 + 0)
    w1stg2 = at("w1stg2", [48, DM], BF16, wk0 + 32768)

    Ps = lambda n, s, d=F32: nc.alloc_psum_tensor(n, s, d).ap()
    pa = [Ps(f"pa{i}", [128, 512]) for i in range(3)]
    pconv = Ps("pconv", [128, 512])
    pcs = Ps("pcs", [128, 512])
    pb = [Ps(f"pb{i}", [128, 512]) for i in range(2)]
    ptr = nc.alloc_psum_tensor("ptr", [128, 1024], BF16).ap()

    sems = {e: nc.alloc_semaphore(f"sem_{e}") for e in ENGS}

    def walk(ctx: Ctx):
        V = lambda fn, waits=(), inc=0: ctx.op("VE", fn, waits, inc)
        Sc = lambda fn, waits=(), inc=0: ctx.op("AC", fn, waits, inc)
        Pe = lambda fn, waits=(), inc=0: ctx.op("PE", fn, waits, inc)
        Sy = lambda fn, waits=(), inc=0: ctx.op("SY", fn, waits, inc)
        Gp = lambda fn, waits=(), inc=0: ctx.op("GP", fn, waits, inc)
        ptrv = [0]

        for ci, cv in enumerate(CONSTS):
            V(lambda e, ci=ci, cv=cv: e.memset(cpool[:, ci:ci + 1], cv),
              inc=(1 if ci == len(CONSTS) - 1 else 0))
        cpool_v = ctx.val("VE")
        Sy(lambda e: e.dma_start(out=iden[:], in_=iden_dram.ap()), inc=16)
        Sy(lambda e: e.dma_start(out=acols[:].rearrange("p (g s) -> p g s", s=16),
                                 in_=alog_d.rearrange("(g p) s -> p g s", p=128)), inc=16)
        for nm, dn in ((dcol, dvec_d), (cbcol, cb_d), (gngc, gng_d), (gnbc, gnb_d)):
            Sy(lambda e, nm=nm, dn=dn: e.dma_start(out=nm[:], in_=dn.rearrange("(g p) -> p g", p=128),
                                                   allow_slow_non_contiguous=True), inc=16)
        Sy(lambda e: e.dma_start(out=cwcol[:].rearrange("p (g k) -> p g k", k=4),
                                 in_=cw_d.rearrange("(g p) o k -> p g (k o)", p=128)), inc=16)
        Gp(lambda e: e.dma_start(out=lngw[:], in_=lng_d.unsqueeze(0).broadcast_to([128, DM])), inc=16)
        Gp(lambda e: e.dma_start(out=lnbw[:], in_=lnb_d.unsqueeze(0).broadcast_to([128, DM])), inc=16)
        ld0 = ctx.val("SY"); gd0 = ctx.val("GP")
        Sc(lambda e: e.activation(acols[:], acols[:], AF.Exp), waits=[("SY", ld0), ("VE", cpool_v)], inc=1)
        V(lambda e: e.tensor_scalar(acols[:], acols[:], -1.0, None, OP.mult),
          waits=[("AC", ctx.val("AC"))], inc=1)

        prev_v = ctx.val("VE")
        for half in range(2):
            Gp(lambda e, h=half: e.dma_start(
                out=w1stg[:].rearrange("p (a d) -> p a d", a=16),
                in_=w1_d[h * 2048:(h + 1) * 2048, :].rearrange("(a p) d -> p a d", p=128)),
               waits=[("VE", prev_v)], inc=16)
            gready = ctx.val("GP")
            for dk in range(8):
                for grp in range(2):
                    ve_evac = max(ctx.val("VE"), ptrv[0])
                    for jj in range(8):
                        j = grp * 8 + jj
                        Pe(lambda e, dk=dk, j=j, jj=jj: e.matmul(
                            ptr[:, jj * 128:(jj + 1) * 128],
                            w1stg[:, j * 1024 + dk * 128:j * 1024 + dk * 128 + 128],
                            iden[:], start=(jj == 0), stop=(jj == 7), is_transpose=True),
                           waits=[("GP", gready), ("VE", ve_evac), ("SY", ld0)],
                           inc=(1 if jj == 7 else 0))
                    pdone = ctx.val("PE")
                    gj = half * 16 + grp * 8
                    if gj < 24:
                        dst = w1ta[:, dk * 3120 + gj * 128: dk * 3120 + (gj + 8) * 128]
                    else:
                        r = gj - 24
                        dst = w1rt[:, dk * 1024 + r * 128: dk * 1024 + (r + 8) * 128]
                    V(lambda e, dst=dst: e.tensor_copy(dst, ptr[:]), waits=[("PE", pdone)], inc=1)
                    ptrv[0] = ctx.val("VE")
            prev_v = ctx.val("VE")
        Gp(lambda e: e.dma_start(out=w1stg2[:], in_=w1_d[4096:4144, :]),
           waits=[("VE", prev_v)], inc=16)
        gready = ctx.val("GP")
        ve_evac = max(ctx.val("VE"), ptrv[0])
        for dk in range(8):
            Pe(lambda e, dk=dk: e.matmul(
                ptr[:, dk * 128:dk * 128 + 48],
                w1stg2[:, dk * 128:(dk + 1) * 128], iden[0:48, 0:48],
                start=(dk == 0), stop=(dk == 7), is_transpose=True),
               waits=[("GP", gready), ("VE", ve_evac)], inc=(1 if dk == 7 else 0))
        pdone = ctx.val("PE")
        V(lambda e: e.tensor_copy(
            w1ta[:].rearrange("p (dk c) -> p dk c", dk=8)[:, :, 3072:3120],
            ptr[:].rearrange("p (dk c) -> p dk c", dk=8)[:, :, 0:48]),
          waits=[("PE", pdone)], inc=1)
        ptrv[0] = ctx.val("VE")
        V(lambda e: e.memset(hstate[:], 0.0), inc=0)
        V(lambda e: e.memset(xbuf[:], 0.0), inc=0)
        V(lambda e: e.memset(ones32[:], 1.0), inc=0)
        V(lambda e: e.memset(onesb[:], 0.0), inc=1)
        for s in range(16):
            V(lambda e, s=s: e.tensor_copy(onesb[:, s * 16 + s:s * 16 + s + 1], cpool[:, 1:2]),
              inc=(1 if s == 15 else 0))
        for g in range(8):
            V(lambda e, g=g: e.tensor_scalar(
                diagD[:, g * 128:(g + 1) * 128], iden[:], dcol[:, g:g + 1], None, OP.mult),
              inc=(1 if g == 7 else 0))

        acx = {}; acxc = {}; acz = {}; acyev = {}; sttv = {}; vpb0 = {}

        def lnstats(reader, sqbuf, wait_extra):
            V(lambda e: e.tensor_reduce(small[:, 16:17], reader, AX.X, OP.add),
              waits=wait_extra, inc=0)
            V(lambda e: e.tensor_tensor(sqbuf, reader, reader, OP.mult), inc=0)
            V(lambda e: e.tensor_reduce(small[:, 17:18], sqbuf, AX.X, OP.add), inc=1)
            _v = ctx.val("VE")
            Sc(lambda e: e.activation(small[:, 12:13], small[:, 16:17], AF.Copy, scale=1.0 / 1024),
               waits=[("VE", _v)], inc=0)
            Sc(lambda e: e.activation(small[:, 18:19], small[:, 17:18], AF.Copy, scale=1.0 / 1024), inc=1)
            _a = ctx.val("AC")
            V(lambda e: e.scalar_tensor_tensor(small[:, 13:14], small[:, 12:13], small[:, 12:13],
                                               small[:, 18:19], OP.mult, OP.subtract),
              waits=[("AC", _a)], inc=1)
            _v = ctx.val("VE")
            Sc(lambda e: e.activation(small[:, 14:15], small[:, 13:14], AF.Ln, scale=-1.0, bias=EPS),
               waits=[("VE", _v)], inc=0)
            Sc(lambda e: e.activation(small[:, 15:16], small[:, 14:15], AF.Exp, scale=-0.5), inc=1)
            return ctx.val("AC")

        for k in range(CH):
            Gp(lambda e, k=k: e.dma_start(
                out=x_bf[:].rearrange("p (s d) -> p s d", s=4),
                in_=u_d[k * T:(k + 1) * T, :].rearrange("(s p) d -> p s d", p=128)),
               waits=[("VE", ctx.val("VE"))], inc=16)
            xld = ctx.val("GP")
            for sub in range(4):
                xs = x_bf[:, sub * 1024:(sub + 1) * 1024]
                w0 = [("GP", xld)] + ([("PE", ctx.val("PE"))] if (k == 0 and sub == 0) else [])
                sc1 = lnstats(xs, spl[:], w0)
                ho = h_outer[:, sub * 1024:(sub + 1) * 1024]
                V(lambda e, xs=xs, ho=ho: e.tensor_scalar(
                    ho, xs, small[:, 12:13], small[:, 15:16], OP.subtract, OP.mult),
                  waits=[("AC", sc1)], inc=1)
                sc1 = lnstats(ho, spl[:], [])
                V(lambda e, ho=ho: e.tensor_scalar(
                    h1stg[:], ho, small[:, 12:13], small[:, 15:16], OP.subtract, OP.mult),
                  waits=[("AC", sc1)], inc=0)
                V(lambda e: e.tensor_tensor(h1stg[:], h1stg[:], lngw[:], OP.mult),
                  waits=[("GP", gd0)], inc=0)
                V(lambda e: e.tensor_tensor(h1stg[:], h1stg[:], lnbw[:], OP.add), inc=1)
                vh1 = ctx.val("VE")
                for dk in range(8):
                    Pe(lambda e, dk=dk: e.matmul(
                        ptr[:, dk * 128:(dk + 1) * 128],
                        h1stg[:, dk * 128:(dk + 1) * 128], iden[:],
                        start=(dk == 0), stop=(dk == 7), is_transpose=True),
                       waits=[("VE", max(vh1, ptrv[0]))], inc=(1 if dk == 7 else 0))
                pdone = ctx.val("PE")
                V(lambda e, sub=sub: e.tensor_copy(
                    h1f[:].rearrange("p (dk t) -> p dk t", dk=8)[:, :, sub * 128:(sub + 1) * 128],
                    ptr[:].rearrange("p (dk t) -> p dk t", dk=8)),
                  waits=[("PE", pdone)], inc=1)
                ptrv[0] = ctx.val("VE")
            h1f_ready = ctx.val("VE")

            for dk in range(8):
                Pe(lambda e, dk=dk: e.matmul(
                    pa[0][0:48, :], w1ta[:, dk * 3120 + 3072:dk * 3120 + 3120],
                    h1f[:, dk * 512:(dk + 1) * 512],
                    start=(dk == 0), stop=(dk == 7)),
                   waits=[("VE", h1f_ready),
                          ("AC", acx[(k - 1, 7)] if k > 0 else 0)],
                   inc=(1 if dk == 7 else 0))
            pbc = ctx.val("PE")
            V(lambda e: e.tensor_copy(bcsg[:], pa[0][0:48, :]),
              waits=[("PE", pbc), ("SY", ctx.val("SY"))], inc=1)
            vbc = ctx.val("VE")
            Sy(lambda e, k=k: e.dma_start(
                out=bc_dram.rearrange("c (s t) -> c s t", t=T)[k, 0:32, :],
                in_=bcsg[0:32, :]), waits=[("VE", vbc)], inc=16)
            bcout = ctx.val("SY")
            Gp(lambda e: e.dma_start(out=spl[0:16, 0:T], in_=bcsg[32:48, :]),
               waits=[("VE", vbc)], inc=16)
            sgin = ctx.val("GP")
            wall_w = [("SY", bcout)] + ([("VE", ctx.tmps_v[(k - 1) * 32 + 31])] if k > 0 else [])
            Gp(lambda e, k=k: e.dma_start(
                out=bwall[:], in_=bc_dram[k, 0:DS * T].unsqueeze(0).broadcast_to([128, DS * T])),
               waits=wall_w, inc=16)
            Gp(lambda e, k=k: e.dma_start(
                out=cwall[:], in_=bc_dram[k, DS * T:2 * DS * T].unsqueeze(0).broadcast_to([128, DS * T])),
               inc=16)
            walls = ctx.val("GP")

            vstt0 = {0: None, 1: None}
            for g in range(8):
                for kk in range(4):
                    V(lambda e, g=g, kk=kk: e.tensor_scalar(
                        diag4[:, kk * 128:(kk + 1) * 128], iden[:],
                        cwcol[:, g * 4 + kk:g * 4 + kk + 1], None, OP.mult),
                      waits=[("PE", ctx.val("PE")), ("SY", ld0)], inc=(1 if kk == 3 else 0))
                vdiag = ctx.val("VE")
                for dk in range(8):
                    Pe(lambda e, g=g, dk=dk: e.matmul(
                        pa[0][:], w1ta[:, dk * 3120 + g * 128:dk * 3120 + (g + 1) * 128],
                        h1f[:, dk * 512:(dk + 1) * 512],
                        start=(dk == 0), stop=(dk == 7)),
                       waits=[("AC", acx[(k, g - 1)] if g > 0 else (acx[(k - 1, 7)] if k > 0 else 0)),
                              ("VE", vbc)],
                       inc=(1 if dk == 7 else 0))
                pxd = ctx.val("PE")
                Sc(lambda e, g=g: e.activation(
                    xbuf[:, g * (T + 3) + 3:(g + 1) * (T + 3)], pa[0][:], AF.Copy),
                   waits=[("PE", pxd)], inc=1)
                acx[(k, g)] = ctx.val("AC")
                for kk in range(4):
                    Pe(lambda e, g=g, kk=kk: e.matmul(
                        pconv[:], diag4[:, kk * 128:(kk + 1) * 128],
                        xbuf[:, g * (T + 3) + kk:g * (T + 3) + kk + T],
                        start=(kk == 0), stop=(kk == 3)),
                       waits=[("AC", max(acx[(k, g)],
                                         acxc[(k, g - 1)] if g > 0 else (acxc[(k - 1, 7)] if k > 0 else 0))),
                              ("VE", vdiag)],
                       inc=(1 if kk == 3 else 0))
                pcv = ctx.val("PE")
                Sc(lambda e, g=g: e.activation(
                    xcg[:, g * T:(g + 1) * T], pconv[:], AF.Silu, bias=cbcol[:, g:g + 1]),
                   waits=[("PE", pcv)], inc=1)
                acxc[(k, g)] = ctx.val("AC")
                for dk in range(8):
                    Pe(lambda e, g=g, dk=dk: e.matmul(
                        pa[2][:], w1ta[:, dk * 3120 + (8 + g) * 128:dk * 3120 + (9 + g) * 128],
                        h1f[:, dk * 512:(dk + 1) * 512],
                        start=(dk == 0), stop=(dk == 7)),
                       waits=[("AC", acz[(k, g - 1)] if g > 0 else (acz[(k - 1, 7)] if k > 0 else 0))],
                       inc=(1 if dk == 7 else 0))
                pz = ctx.val("PE")
                Sc(lambda e, g=g: e.activation(szst[:], pa[2][:], AF.Silu),
                   waits=[("PE", pz), ("SY", ctx.val("SY"))], inc=1)
                acz[(k, g)] = ctx.val("AC")
                Sy(lambda e, g=g, k=k: e.dma_start(
                    out=sz_dram.rearrange("(g p) t -> p g t", p=128)[:, g, k * T:(k + 1) * T],
                    in_=szst[:]), waits=[("AC", acz[(k, g)])], inc=16)
                if g == 0:
                    Sc(lambda e: e.activation(sgsig[0:16, 0:T], spl[0:16, 0:T], AF.Tanh, scale=0.5),
                       waits=[("GP", sgin)], inc=1)
                    tgh = ctx.val("AC")
                    def _sgf(e):
                        with nc.allow_low_precision(reason="sigmoid gate stored bf16"):
                            return e.tensor_scalar(sgsig[0:16, 0:T], sgsig[0:16, 0:T],
                                                   0.5, 0.5, OP.mult, OP.add)
                    V(_sgf, waits=[("AC", tgh)], inc=1)
                sub, oh = g // 2, g % 2
                ps = pb[oh]
                if vstt0[oh] is None:
                    w_ps = [("AC", acyev[(k - 1, 6 + oh)])] if k > 0 else []
                else:
                    w_ps = [("VE", vstt0[oh])]
                for dk in range(8):
                    Pe(lambda e, sub=sub, oh=oh, dk=dk, ps=ps: e.matmul(
                        ps[:], h1f[:, dk * 512 + sub * 128:dk * 512 + (sub + 1) * 128],
                        w1rt[:, dk * 1024 + oh * 512:dk * 1024 + (oh + 1) * 512],
                        start=(dk == 0), stop=(dk == 7)),
                       waits=w_ps, inc=(1 if dk == 7 else 0))
                pr = ctx.val("PE")
                V(lambda e, sub=sub, oh=oh, ps=ps: e.scalar_tensor_tensor(
                    r2st[:], ps[:], 0.0,
                    h_outer[:, sub * 1024 + oh * 512:sub * 1024 + (oh + 1) * 512],
                    OP.bypass, OP.add),
                  waits=[("PE", pr), ("SY", ctx.val("SY"))], inc=0)
                V(lambda e, sub=sub, oh=oh: e.tensor_tensor(
                    r2st[:], r2st[:],
                    x_bf[:, sub * 1024 + oh * 512:sub * 1024 + (oh + 1) * 512], OP.add), inc=1)
                vstt0[oh] = ctx.val("VE")
                Sy(lambda e, k=k, sub=sub, oh=oh: e.dma_start(
                    out=r2_dram.rearrange("(a p) d -> p a d", p=128)[:, k * 4 + sub, oh * 512:(oh + 1) * 512],
                    in_=r2st[:]), waits=[("VE", vstt0[oh])], inc=16)
            vr2k = ctx.val("VE")

            QT = 4 * T

            def fin_ac(gf):
                Sc(lambda e, gf=gf: e.activation(dtxg[:, gf * T:(gf + 1) * T], pb[gf % 2][:], AF.Copy),
                   waits=[("PE", ctx.yinj_p[k * 32 + gf * 4 + 3])], inc=1)
                acyev[(k, gf)] = ctx.val("AC")

            def fin_vesy(gf):
                yv = dtxg[:, gf * T:(gf + 1) * T]
                V(lambda e, gf=gf, k=k, yv=yv: e.tensor_reduce(
                    ystats[:, gf * CH + k:gf * CH + k + 1], yv, AX.X, OP.add),
                  waits=[("AC", acyev[(k, gf)])], inc=0)
                V(lambda e, yv=yv: e.tensor_tensor(ubuf[:, 0:T], yv, yv, OP.mult), inc=0)
                V(lambda e, gf=gf, k=k: e.tensor_reduce(
                    ystats[:, 64 + gf * CH + k:64 + gf * CH + k + 1],
                    ubuf[:, 0:T], AX.X, OP.add), inc=1)
                vyst = ctx.val("VE")
                Sy(lambda e, gf=gf, k=k, yv=yv: e.dma_start(
                    out=y_dram.rearrange("(g p) t -> p g t", p=128)[:, gf, k * T:(k + 1) * T],
                    in_=yv), waits=[("VE", vyst)], inc=16)

            def softplus(gi, pdt):
                sl2 = (gi % 2) * T
                Sc(lambda e, sl2=sl2: e.activation(spl[:, sl2:sl2 + T], pa[1][:], AF.Abs),
                   waits=[("PE", pdt)], inc=0)
                Sc(lambda e, sl2=sl2: e.activation(spl[:, sl2:sl2 + T], spl[:, sl2:sl2 + T], AF.Exp, scale=-1.0), inc=0)
                Sc(lambda e, sl2=sl2: e.activation(spl[:, sl2:sl2 + T], spl[:, sl2:sl2 + T], AF.Ln, bias=1.0), inc=1)
                return ctx.val("AC")

            acdt = None
            for gi in range(9):
                pdt = None
                if gi < 8:
                    w_dt = ([("VE", sttv[(k, gi - 1)])] if gi > 0 else
                            ([("VE", sttv[(k - 1, 7)])] if k > 0 else []))
                    for dk in range(8):
                        Pe(lambda e, gi=gi, dk=dk: e.matmul(
                            pa[1][:], w1ta[:, dk * 3120 + (16 + gi) * 128:dk * 3120 + (17 + gi) * 128],
                            h1f[:, dk * 512:(dk + 1) * 512],
                            start=(dk == 0), stop=(dk == 7)),
                           waits=(w_dt if dk == 0 else []), inc=(1 if dk == 7 else 0))
                    pdt = ctx.val("PE")
                    if gi == 0:
                        acdt = softplus(0, pdt)
                if gi >= 1:
                    sg = gi - 1
                    if sg >= 2:
                        w_prev = [("AC", acyev[(k, sg - 2)])]
                    elif k > 0:
                        w_prev = [("AC", acyev[(k - 1, 6 + sg)]), ("VE", vstt0[sg % 2])]
                    else:
                        w_prev = [("VE", vstt0[sg % 2])]
                    Pe(lambda e, sg=sg: e.matmul(
                        pb[sg % 2][:], diagD[:, sg * 128:(sg + 1) * 128], xcg[:, sg * T:(sg + 1) * T],
                        start=True, stop=False),
                       waits=[("AC", acxc[(k, sg)])] + w_prev, inc=1)
                    for q in range(4):
                        jq = k * 32 + sg * 4 + q
                        par = (jq % 2) * QT
                        dsl = dabuf[:, par:par + QT]
                        hsl = hbuf[:, par:par + QT]
                        tsl = tmps[:, par:par + QT]
                        w_sc = [("VE", max(ctx.scan_v[jq - 2], vr2k))] if jq >= 2 else [("VE", vr2k)]
                        for m in range(4):
                            j = sg * 16 + q * 4 + m
                            Sc(lambda e, sg=sg, j=j, m=m, d=dsl: e.activation(
                                d[:, m * T:(m + 1) * T], dtg[:, sg * T:(sg + 1) * T],
                                AF.Exp, scale=acols[:, j:j + 1]),
                               waits=(w_sc if m == 0 else []) + [("VE", sttv[(k, sg)])],
                               inc=(1 if m == 3 else 0))
                        acda = ctx.val("AC")
                        if q == 1:
                            if gi < 8:
                                acdt = softplus(gi, pdt)
                            if sg >= 1:
                                fin_ac(sg - 1)
                        V(lambda e, sg=sg, q=q: e.tensor_tensor(
                            ubuf[:].rearrange("p (m t) -> p m t", m=4),
                            dtxg[:, sg * T:(sg + 1) * T].unsqueeze(1).broadcast_to([128, 4, T]),
                            bwall[:, q * QT:(q + 1) * QT].rearrange("p (m t) -> p m t", m=4), OP.mult),
                          waits=[("GP", walls)], inc=0)
                        for m in range(1, 4):
                            j = sg * 16 + q * 4 + m
                            V(lambda e, m=m, j=j, d=dsl: e.scalar_tensor_tensor(
                                ubuf[:, m * T:m * T + 1], d[:, m * T:m * T + 1],
                                hstate[:, j:j + 1], ubuf[:, m * T:m * T + 1], OP.mult, OP.add),
                              waits=[("AC", acda)], inc=0)
                            V(lambda e, m=m, d=dsl: e.memset(d[:, m * T:m * T + 1], 0.0), inc=0)
                        w_h = [("PE", ctx.cs_p[jq - 2])] if jq >= 2 else []
                        V(lambda e, q=q, sg=sg, d=dsl, hs=hsl: e.tensor_tensor_scan(
                            hs, d, ubuf[:], hstate[:, sg * 16 + q * 4:sg * 16 + q * 4 + 1],
                            OP.mult, OP.add),
                          waits=[("AC", acda)] + w_h, inc=1)
                        ctx.scan_v[jq] = ctx.val("VE")
                        w_t = [("PE", ctx.yinj_p[jq - 2])] if jq >= 2 else []
                        V(lambda e, q=q, hs=hsl, ts=tsl: e.tensor_tensor(
                            ts, hs, cwall[:, q * QT:(q + 1) * QT], OP.mult), waits=w_t, inc=1)
                        ctx.tmps_v[jq] = ctx.val("VE")
                        V(lambda e, sg=sg, q=q, hs=hsl: e.tensor_copy(
                            hstate[:, sg * 16 + q * 4:sg * 16 + q * 4 + 4],
                            hs[:].rearrange("p (m t) -> p m t", m=4)[:, :, T - 1:T].rearrange("p m o -> p (m o)")),
                          inc=0)
                        for m in range(4):
                            j = sg * 16 + q * 4 + m
                            Pe(lambda e, j=j, m=m, hs=hsl: e.matmul(
                                pcs[0:16, :], onesb[:, (j % 16) * 16:(j % 16) * 16 + 16],
                                hs[:, m * T:(m + 1) * T],
                                start=(j == 0), stop=(j == 127)),
                               waits=([("VE", ctx.scan_v[jq])] if m == 0 else []), inc=(1 if m == 3 else 0))
                        ctx.cs_p[jq] = ctx.val("PE")
                        for m in range(4):
                            Pe(lambda e, sg=sg, q=q, m=m, ts=tsl: e.matmul(
                                pb[sg % 2][:], iden[:], ts[:, m * T:(m + 1) * T],
                                start=False, stop=(q == 3 and m == 3)),
                               waits=([("VE", ctx.tmps_v[jq])] if m == 0 else []), inc=(1 if m == 3 else 0))
                        ctx.yinj_p[jq] = ctx.val("PE")
                if gi < 8:
                    sl2 = (gi % 2) * T
                    V(lambda e, gi=gi, sl2=sl2: e.scalar_tensor_tensor(
                        dtg[:, gi * T:(gi + 1) * T], pa[1][:], 0.0, spl[:, sl2:sl2 + T], OP.max, OP.add),
                      waits=[("AC", acdt)], inc=1)
                    sttv[(k, gi)] = ctx.val("VE")
                    V(lambda e, gi=gi: e.tensor_tensor(
                        dtxg[:, gi * T:(gi + 1) * T], dtg[:, gi * T:(gi + 1) * T],
                        xcg[:, gi * T:(gi + 1) * T], OP.mult),
                      waits=[("AC", acxc[(k, gi)]), ("SY", ctx.val("SY"))], inc=1)
                if gi >= 2:
                    fin_vesy(gi - 2)
            fin_ac(7)
            fin_vesy(7)

            V(lambda e, k=k: e.scalar_tensor_tensor(
                sseq[0:16, k * T:(k + 1) * T], pcs[0:16, :], 1.0 / DM,
                sgsig[0:16, 0:T], OP.mult, OP.mult),
              waits=[("PE", ctx.cs_p[k * 32 + 31])], inc=1)

            for g in range(8):
                V(lambda e, g=g: e.tensor_copy(
                    xbuf[:, g * (T + 3):g * (T + 3) + 3],
                    xbuf[:, g * (T + 3) + T:g * (T + 3) + T + 3]),
                  waits=[("PE", ctx.val("PE"))], inc=(1 if g == 7 else 0))

        if 'G' not in phases and 'B' not in phases and 'C' not in phases and 'D' not in phases:
            fy = ctx.val('SY')
            Sy(lambda e: e.wait_ge(sems['SY'], fy), inc=0)
            return
        for g in range(8):
            V(lambda e, g=g: e.tensor_reduce(
                small[:, 2 * g:2 * g + 1], ystats[:, g * CH:(g + 1) * CH], AX.X, OP.add), inc=0)
            V(lambda e, g=g: e.tensor_reduce(
                small[:, 2 * g + 1:2 * g + 2], ystats[:, 64 + g * CH:64 + (g + 1) * CH], AX.X, OP.add),
              inc=(1 if g == 7 else 0))
        vgs = ctx.val("VE")
        Pe(lambda e: e.matmul(pcs[0:1, 0:16], ones32[:], small[:, 0:16], start=True, stop=True),
           waits=[("VE", vgs)], inc=1)
        pgs = ctx.val("PE")
        V(lambda e: e.tensor_copy(small[0:1, 16:32], pcs[0:1, 0:16]), waits=[("PE", pgs)], inc=1)
        _v = ctx.val("VE")
        p2 = lambda i: small[0:1, 16:32].rearrange("p (a b) -> p a b", b=2)[:, :, i:i + 1]
        NT = 128.0 * L
        Sc(lambda e: e.activation(small[0:1, 32:40].unsqueeze(2), p2(0), AF.Copy, scale=1.0 / NT),
           waits=[("VE", _v)], inc=0)
        Sc(lambda e: e.activation(small[0:1, 40:48].unsqueeze(2), p2(1), AF.Copy, scale=1.0 / NT), inc=0)
        Sc(lambda e: e.activation(small[0:1, 48:56], small[0:1, 32:40], AF.Square), inc=1)
        _a = ctx.val("AC")
        V(lambda e: e.tensor_tensor(small[0:1, 56:64], small[0:1, 48:56], small[0:1, 40:48], OP.subtract),
          waits=[("AC", _a)], inc=1)
        _v = ctx.val("VE")
        Sc(lambda e: e.activation(small[0:1, 48:56], small[0:1, 56:64], AF.Ln, scale=-1.0, bias=EPS),
           waits=[("VE", _v)], inc=0)
        Sc(lambda e: e.activation(small[0:1, 56:64], small[0:1, 48:56], AF.Exp, scale=-0.5), inc=1)
        acg = ctx.val("AC")
        Sy(lambda e: e.dma_start(out=gn_dram[0:8].unsqueeze(0), in_=small[0:1, 32:40]),
           waits=[("AC", acg)], inc=16)
        Sy(lambda e: e.dma_start(out=gn_dram[8:16].unsqueeze(0), in_=small[0:1, 56:64]), inc=16)
        sgn = ctx.val("SY")
        Gp(lambda e: e.dma_start(out=gmw[:], in_=gn_dram[0:8].unsqueeze(0).broadcast_to([128, 8])),
           waits=[("SY", sgn)], inc=16)
        Gp(lambda e: e.dma_start(out=rsw[:], in_=gn_dram[8:16].unsqueeze(0).broadcast_to([128, 8])), inc=16)
        ggn = ctx.val("GP")
        V(lambda e: e.tensor_tensor(alpha[:], gngc[:], rsw[:], OP.mult), waits=[("GP", ggn)], inc=1)
        _v = ctx.val("VE")
        Sc(lambda e: e.activation(gstat[:, 0:8], alpha[:], AF.Copy), waits=[("VE", _v)], inc=1)
        _a = ctx.val("AC")
        V(lambda e: e.tensor_tensor(small[:, 0:8], gmw[:], gstat[:, 0:8], OP.mult),
          waits=[("AC", _a)], inc=1)
        _v = ctx.val("VE")
        Sc(lambda e: e.activation(gstat[:, 8:16], small[:, 0:8], AF.Copy, scale=-1.0),
           waits=[("VE", _v)], inc=1)
        _a = ctx.val("AC")
        V(lambda e: e.scalar_tensor_tensor(beta[:], gstat[:, 8:16], 1.0, gnbc[:], OP.bypass, OP.add),
          waits=[("AC", _a)], inc=1)
        Gp(lambda e: e.dma_start(out=w2stg[:].rearrange("p (a f) -> p a f", a=8),
                                 in_=w2_d.rearrange("(a p) f -> p a f", p=128)),
           waits=[("PE", ctx.val("PE")), ("VE", ctx.val("VE"))], inc=16)
        gw2 = ctx.val("GP")
        for fk in range(9):
            vprev = max(ctx.val("VE"), ptrv[0])
            for a in range(8):
                if fk < 8:
                    src = w2stg[:, a * 1040 + fk * 128:a * 1040 + (fk + 1) * 128]
                    dstp = ptr[:, a * 128:(a + 1) * 128]
                else:
                    src = w2stg[:, a * 1040 + 1024:a * 1040 + 1040]
                    dstp = ptr[0:16, a * 128:(a + 1) * 128]
                Pe(lambda e, src=src, dstp=dstp, a=a: e.matmul(
                    dstp, src, iden[:], start=(a == 0), stop=(a == 7), is_transpose=True),
                   waits=[("GP", gw2), ("VE", vprev)], inc=(1 if a == 7 else 0))
            pw2 = ctx.val("PE")
            if fk < 8:
                V(lambda e, fk=fk: e.tensor_copy(w2t[:, fk * 1024:(fk + 1) * 1024], ptr[:]),
                  waits=[("PE", pw2)], inc=1)
            else:
                V(lambda e, fk=fk: e.tensor_copy(w2t[0:16, fk * 1024:(fk + 1) * 1024], ptr[0:16, :]),
                  waits=[("PE", pw2)], inc=1)
            ptrv[0] = ctx.val("VE")

        def quant(src, dst, w, waits=()):
            nb = w // 16
            V(lambda e: e.tensor_reduce(
                mxs[:, 0:nb].unsqueeze(2), src.rearrange("p (b x) -> p b x", x=16),
                AX.X, OP.max, apply_absolute_value=True), waits=list(waits), inc=1)
            _v = ctx.val("VE")
            Sc(lambda e: e.activation(mlnt[:, 0:nb], mxs[:, 0:nb], AF.Ln, scale=1.0 / 6.0, bias=1e-12),
               waits=[("VE", _v)], inc=0)
            Sc(lambda e: e.activation(mrscl[:, 0:nb], mlnt[:, 0:nb], AF.Exp, scale=-1.0), inc=0)
            Sc(lambda e: e.activation(msclb[:, 0:nb], mxs[:, 0:nb], AF.Copy, scale=1.0 / 6.0), inc=1)
            _a = ctx.val("AC")
            V(lambda e: e.tensor_tensor(
                qxn[:, 0:w].rearrange("p (b x) -> p b x", x=16),
                src.rearrange("p (b x) -> p b x", x=16),
                mrscl[:, 0:nb].unsqueeze(2).broadcast_to([128, nb, 16]), OP.mult),
              waits=[("AC", _a)], inc=0)
            V(lambda e: e.tensor_scalar(qtf[:, 0:w], qxn[:, 0:w], 2.0, -2.0, OP.min, OP.max), inc=0)
            V(lambda e: e.tensor_scalar(qacc[:, 0:w], qtf[:, 0:w], MH, MH, OP.add, OP.subtract), inc=0)
            V(lambda e: e.tensor_scalar(qtf[:, 0:w], qxn[:, 0:w], 2.0, 4.0, OP.max, OP.min), inc=0)
            V(lambda e: e.tensor_scalar(qbr[:, 0:w], qtf[:, 0:w], MI, MI + 2.0, OP.add, OP.subtract), inc=0)
            V(lambda e: e.tensor_tensor(qacc[:, 0:w], qacc[:, 0:w], qbr[:, 0:w], OP.add), inc=0)
            V(lambda e: e.tensor_scalar(qtf[:, 0:w], qxn[:, 0:w], -2.0, -4.0, OP.min, OP.max), inc=0)
            V(lambda e: e.tensor_scalar(qbr[:, 0:w], qtf[:, 0:w], MI, MI - 2.0, OP.add, OP.subtract), inc=0)
            V(lambda e: e.tensor_tensor(qacc[:, 0:w], qacc[:, 0:w], qbr[:, 0:w], OP.add), inc=0)
            V(lambda e: e.tensor_scalar(qbr[:, 0:w], qxn[:, 0:w], 5.0, 2.0, OP.is_gt, OP.mult), inc=0)
            V(lambda e: e.tensor_tensor(qacc[:, 0:w], qacc[:, 0:w], qbr[:, 0:w], OP.add), inc=0)
            V(lambda e: e.tensor_scalar(qbr[:, 0:w], qxn[:, 0:w], -5.0, -2.0, OP.is_lt, OP.mult), inc=0)
            V(lambda e: e.tensor_tensor(qacc[:, 0:w], qacc[:, 0:w], qbr[:, 0:w], OP.add), inc=0)
            V(lambda e: e.tensor_tensor(
                dst.rearrange("p (b x) -> p b x", x=16),
                qacc[:, 0:w].rearrange("p (b x) -> p b x", x=16),
                msclb[:, 0:nb].unsqueeze(2).broadcast_to([128, nb, 16]), OP.mult), inc=1)

        if 'B' not in phases:
            fy = ctx.val('SY')
            Sy(lambda e: e.wait_ge(sems['SY'], fy), inc=0)
            return
        pwjobs = (([("g", i) for i in range(22)] + [("u", i) for i in range(22)]) if 'W' in phases else []) \
                 + ([("d", i) for i in range(8)] if 'X' in phases else [])
        for kb in range(CH):
            vdone = ctx.val("VE")
            Sy(lambda e, kb=kb: e.dma_start(
                out=yb[:].rearrange("p (g t) -> p g t", g=8),
                in_=y_dram.rearrange("(g p) t -> p g t", p=128)[:, :, kb * T:(kb + 1) * T]),
               waits=[("VE", vdone)], inc=16)
            Sy(lambda e, kb=kb: e.dma_start(
                out=szb2[:].rearrange("p (g t) -> p g t", g=8),
                in_=sz_dram.rearrange("(g p) t -> p g t", p=128)[:, :, kb * T:(kb + 1) * T]), inc=16)
            Sy(lambda e, kb=kb: e.dma_start(
                out=r2rb[:].rearrange("p (a d) -> p a d", a=4),
                in_=r2_dram[kb * T:(kb + 1) * T, :].rearrange("(a p) d -> p a d", p=128)), inc=16)
            syld = ctx.val("SY")
            for g in range(8):
                V(lambda e, g=g: e.tensor_scalar(
                    y2f[:, g * T:(g + 1) * T], yb[:, g * T:(g + 1) * T],
                    alpha[:, g:g + 1], beta[:, g:g + 1], OP.mult, OP.add),
                  waits=[("SY", syld)], inc=0)
                V(lambda e, g=g: e.tensor_tensor(
                    y2f[:, g * T:(g + 1) * T], y2f[:, g * T:(g + 1) * T],
                    szb2[:, g * T:(g + 1) * T], OP.mult), inc=(1 if g == 7 else 0))
            vy2 = ctx.val("VE")
            vstt = {0: None, 1: None}
            for sub in range(4):
                for oh in range(2):
                    ps = pb[oh]
                    vprev = vstt[oh] if vstt[oh] is not None else ctx.val("VE")
                    for fk in range(9):
                        if fk < 8:
                            stat = y2f[:, fk * T + sub * 128:fk * T + (sub + 1) * 128]
                            mov = w2t[:, fk * 1024 + oh * 512:fk * 1024 + (oh + 1) * 512]
                        else:
                            stat = sseq[0:16, kb * T + sub * 128:kb * T + (sub + 1) * 128]
                            mov = w2t[0:16, 8 * 1024 + oh * 512:8 * 1024 + (oh + 1) * 512]
                        Pe(lambda e, stat=stat, mov=mov, fk=fk, ps=ps: e.matmul(
                            ps[:], stat, mov, start=(fk == 0), stop=(fk == 8)),
                           waits=[("VE", max(vy2, vprev))], inc=(1 if fk == 8 else 0))
                    po = ctx.val("PE")
                    V(lambda e, sub=sub, oh=oh, ps=ps: e.scalar_tensor_tensor(
                        x1st[:, oh * T:(oh + 1) * T], ps[:], 0.0,
                        r2rb[:, sub * 1024 + oh * 512:sub * 1024 + (oh + 1) * 512],
                        OP.bypass, OP.add),
                      waits=[("PE", po), ("SY", ctx.val("SY"))], inc=1)
                    vstt[oh] = ctx.val("VE")
                    vx1 = ctx.val("VE")
                    Sy(lambda e, kb=kb, sub=sub, oh=oh: e.dma_start(
                        out=x1_dram.rearrange("(a p) d -> p a d", p=128)[:, kb * 4 + sub, oh * 512:(oh + 1) * 512],
                        in_=x1st[:, oh * T:(oh + 1) * T]), waits=[("VE", vx1)], inc=16)
            for (wt, i) in (pwjobs[kb * len(pwjobs) // CH:(kb + 1) * len(pwjobs) // CH] if pwjobs else []):
                if wt in ("g", "u"):
                    wsrc = wg_d if wt == "g" else wu_d
                    wdst = wgq if wt == "g" else wuq
                    vprev = ctx.val("VE")
                    Sy(lambda e, wsrc=wsrc, i=i: e.dma_start(
                        out=wraw[:, 0:1024], in_=wsrc[i * 128:(i + 1) * 128, :]),
                       waits=[("VE", vprev)], inc=16)
                    sld = ctx.val("SY")
                    quant(wraw[:, 0:1024], qnat[:, 0:1024], 1024, waits=[("SY", sld)])
                    vq = max(ctx.val("VE"), ptrv[0])
                    for dk in range(8):
                        Pe(lambda e, dk=dk: e.matmul(
                            ptr[:, dk * 128:(dk + 1) * 128], qnat[:, dk * 128:(dk + 1) * 128],
                            iden[:], start=(dk == 0), stop=(dk == 7), is_transpose=True),
                           waits=[("VE", vq)], inc=(1 if dk == 7 else 0))
                    pq = ctx.val("PE")
                    V(lambda e, wdst=wdst, i=i: e.tensor_copy(
                        wdst.rearrange("p (dk c) -> p dk c", dk=8)[:, :, i * 128:(i + 1) * 128],
                        ptr[:].rearrange("p (dk c) -> p dk c", dk=8)),
                      waits=[("PE", pq)], inc=1)
                    ptrv[0] = ctx.val("VE")
                else:
                    vprev = ctx.val("VE")
                    Sy(lambda e, i=i: e.dma_start(out=wraw[:], in_=wd_d[i * 128:(i + 1) * 128, :]),
                       waits=[("VE", vprev)], inc=16)
                    sld = ctx.val("SY")
                    quant(wraw[:, 0:1408], qnat[:, 0:1408], 1408, waits=[("SY", sld)])
                    quant(wraw[:, 1408:2816], qnat[:, 1408:2816], 1408)
                    vq = max(ctx.val("VE"), ptrv[0])
                    for b0 in range(0, 22, 8):
                        nbt = min(8, 22 - b0)
                        vpr = ctx.val("VE")
                        for t in range(nbt):
                            nk = b0 + t
                            Pe(lambda e, nk=nk, t=t, nbt=nbt: e.matmul(
                                ptr[:, t * 128:(t + 1) * 128], qnat[:, nk * 128:(nk + 1) * 128],
                                iden[:], start=(t == 0), stop=(t == nbt - 1), is_transpose=True),
                               waits=[("VE", max(vq, vpr))], inc=(1 if t == nbt - 1 else 0))
                        pq = ctx.val("PE")
                        V(lambda e, b0=b0, nbt=nbt: e.tensor_copy(
                            wdstg[:, b0 * 128:(b0 + nbt) * 128], ptr[:, 0:nbt * 128]),
                          waits=[("PE", pq), ("SY", ctx.val("SY"))], inc=1)
                        ptrv[0] = ctx.val("VE")
                    vqd = ctx.val("VE")
                    Sy(lambda e, i=i: e.dma_start(
                        out=wdq_dram.rearrange("p (nk c) -> p nk c", nk=22)[:, :, i * 128:(i + 1) * 128],
                        in_=wdstg[:].rearrange("p (nk c) -> p nk c", nk=22)),
                       waits=[("VE", vqd)], inc=16)

        if 'C' not in phases:
            fy = ctx.val('SY')
            Sy(lambda e: e.wait_ge(sems['SY'], fy), inc=0)
            return
        for kc in range(CH):
            vdone = ctx.val("VE")
            Sy(lambda e, kc=kc: e.dma_start(
                out=x1r[:].rearrange("p (a d) -> p a d", a=4),
                in_=x1_dram[kc * T:(kc + 1) * T, :].rearrange("(a p) d -> p a d", p=128)),
               waits=[("VE", vdone)], inc=16)
            sx1 = ctx.val("SY")
            for it in range(5):
                if it < 4:
                    sub = it
                    par = sub % 2
                    xs = x1r[:, sub * 1024:(sub + 1) * 1024]
                    V(lambda e, xs=xs: e.tensor_reduce(small[:, 16:17], xs, AX.X, OP.add),
                      waits=[("SY", sx1)], inc=0)
                    V(lambda e, xs=xs: e.tensor_tensor(qtf[:, 0:1024], xs, xs, OP.mult), inc=0)
                    V(lambda e: e.tensor_reduce(small[:, 17:18], qtf[:, 0:1024], AX.X, OP.add), inc=1)
                    _v = ctx.val("VE")
                    Sc(lambda e: e.activation(small[:, 12:13], small[:, 16:17], AF.Copy, scale=1.0 / 1024),
                       waits=[("VE", _v)], inc=0)
                    Sc(lambda e: e.activation(small[:, 18:19], small[:, 17:18], AF.Copy, scale=1.0 / 1024), inc=1)
                    _a = ctx.val("AC")
                    V(lambda e: e.scalar_tensor_tensor(small[:, 13:14], small[:, 12:13], small[:, 12:13],
                                                       small[:, 18:19], OP.mult, OP.subtract),
                      waits=[("AC", _a)], inc=1)
                    _v = ctx.val("VE")
                    Sc(lambda e: e.activation(small[:, 14:15], small[:, 13:14], AF.Ln, scale=-1.0, bias=EPS),
                       waits=[("VE", _v)], inc=0)
                    Sc(lambda e: e.activation(small[:, 15:16], small[:, 14:15], AF.Exp, scale=-0.5), inc=1)
                    sc1 = ctx.val("AC")
                    V(lambda e, xs=xs: e.tensor_scalar(
                        xnorm[:], xs, small[:, 12:13], small[:, 15:16], OP.subtract, OP.mult),
                      waits=[("AC", sc1)], inc=1)
                    quant(xnorm[:], xq2[:, par * 1024:(par + 1) * 1024], 1024)
                    ctx.pca_q[(kc, sub)] = ctx.val("VE")
                if it >= 1:
                    sub = it - 1
                    par = sub % 2
                    xqt = xqT2[:, par * 1024:(par + 1) * 1024]
                    vxqt = ctx.pca_t[(kc, sub)]
                    ac_sil = {}
                    ve_hm = {}
                    for nb in range(6):
                        noff = nb * 512
                        nw = min(512, FI - noff)
                        pg_ = [pa[0], pa[2]][nb % 2]
                        pu_ = [pa[1], pconv][nb % 2]
                        w_g = [("VE", vxqt), ("AC", ac_sil.get(nb - 2, ctx.val("AC") if nb < 2 else 0))]
                        if nb < 2:
                            w_g = [("VE", vxqt), ("AC", ctx.val("AC"))]
                        else:
                            w_g = [("VE", vxqt), ("AC", ac_sil[nb - 2])]
                        for dk in range(8):
                            Pe(lambda e, dk=dk, noff=noff, nw=nw, xqt=xqt, pg_=pg_: e.matmul(
                                pg_[:, 0:nw], xqt[:, dk * 128:(dk + 1) * 128],
                                wgq[:, dk * FI + noff:dk * FI + noff + nw],
                                start=(dk == 0), stop=(dk == 7)),
                               waits=w_g, inc=(1 if dk == 7 else 0))
                        w_u = [("VE", ve_hm[nb - 2])] if nb >= 2 else [("VE", vxqt)]
                        for dk in range(8):
                            Pe(lambda e, dk=dk, noff=noff, nw=nw, xqt=xqt, pu_=pu_: e.matmul(
                                pu_[:, 0:nw], xqt[:, dk * 128:(dk + 1) * 128],
                                wuq[:, dk * FI + noff:dk * FI + noff + nw],
                                start=(dk == 0), stop=(dk == 7)),
                               waits=w_u, inc=(1 if dk == 7 else 0))
                        pgu = ctx.val("PE")
                        Sc(lambda e, nw=nw, pg_=pg_: e.activation(gsil[:, 0:nw], pg_[:, 0:nw], AF.Silu),
                           waits=[("PE", pgu)], inc=1)
                        ac_sil[nb] = ctx.val("AC")
                        V(lambda e, noff=noff, nw=nw, pu_=pu_: e.tensor_tensor(
                            hmid[:, noff:noff + nw], gsil[:, 0:nw], pu_[:, 0:nw], OP.mult),
                          waits=[("AC", ac_sil[nb])], inc=1)
                        ve_hm[nb] = ctx.val("VE")
                    quant(hmid[:, 0:1408], h2q[:, 0:1408], 1408)
                    quant(hmid[:, 1408:2816], h2q[:, 1408:2816], 1408)
                    vq2 = ctx.val("VE")
                    for b0 in range(0, 22, 8):
                        nbt = min(8, 22 - b0)
                        vpr = max(ctx.val("VE"), ptrv[0])
                        for t in range(nbt):
                            nk = b0 + t
                            Pe(lambda e, nk=nk, t=t, nbt=nbt: e.matmul(
                                ptr[:, t * 128:(t + 1) * 128], h2q[:, nk * 128:(nk + 1) * 128],
                                iden[:], start=(t == 0), stop=(t == nbt - 1), is_transpose=True),
                               waits=[("VE", max(vq2, vpr))], inc=(1 if t == nbt - 1 else 0))
                        ph = ctx.val("PE")
                        V(lambda e, b0=b0, nbt=nbt: e.tensor_copy(
                            h2qT[:, b0 * 128:(b0 + nbt) * 128], ptr[:, 0:nbt * 128]),
                          waits=[("PE", ph), ("SY", ctx.val("SY"))], inc=1)
                        ptrv[0] = ctx.val("VE")
                    vsp = ctx.val("VE")
                    Sy(lambda e, kc=kc, sub=sub: e.dma_start(
                        out=h2qt_dram.rearrange("(nk p) t -> p nk t", p=128)[:, :, kc * T + sub * 128:kc * T + (sub + 1) * 128],
                        in_=h2qT[:].rearrange("p (nk c) -> p nk c", nk=22)),
                       waits=[("VE", vsp)], inc=16)
                if it < 4:
                    sub = it
                    par = sub % 2
                    vq = max(ctx.pca_q[(kc, sub)], ptrv[0])
                    for dk in range(8):
                        Pe(lambda e, dk=dk, par=par: e.matmul(
                            ptr[:, dk * 128:(dk + 1) * 128],
                            xq2[:, par * 1024 + dk * 128:par * 1024 + (dk + 1) * 128],
                            iden[:], start=(dk == 0), stop=(dk == 7), is_transpose=True),
                           waits=[("VE", vq)], inc=(1 if dk == 7 else 0))
                    pxq = ctx.val("PE")
                    V(lambda e, par=par: e.tensor_copy(xqT2[:, par * 1024:(par + 1) * 1024], ptr[:]),
                      waits=[("PE", pxq)], inc=1)
                    ptrv[0] = ctx.val("VE")
                    ctx.pca_t[(kc, sub)] = ctx.val("VE")

        if 'D' not in phases:
            fy = ctx.val('SY')
            Sy(lambda e: e.wait_ge(sems['SY'], fy), inc=0)
            return
        vdone = ctx.val("VE")
        Sy(lambda e: e.dma_start(out=wdqt[:], in_=wdq_dram[:, :]),
           waits=[("VE", vdone), ("PE", ctx.val("PE"))], inc=16)
        swd = ctx.val("SY")
        for kd in range(CH):
            vdone = ctx.val("VE")
            Sy(lambda e, kd=kd: e.dma_start(
                out=h2qtc[:].rearrange("p (nk t) -> p nk t", nk=22),
                in_=h2qt_dram.rearrange("(nk p) t -> p nk t", p=128)[:, :, kd * T:(kd + 1) * T]),
               waits=[("VE", vdone)], inc=16)
            Sy(lambda e, kd=kd: e.dma_start(
                out=x1r2[:].rearrange("p (a d) -> p a d", a=4),
                in_=x1_dram[kd * T:(kd + 1) * T, :].rearrange("(a p) d -> p a d", p=128)), inc=16)
            sld = ctx.val("SY")
            vstt = {0: None, 1: None}
            for sub in range(4):
                for oh in range(2):
                    ps = pb[oh]
                    vprev = vstt[oh] if vstt[oh] is not None else ctx.val("VE")
                    for nk in range(22):
                        Pe(lambda e, sub=sub, oh=oh, nk=nk, ps=ps: e.matmul(
                            ps[:], h2qtc[:, nk * T + sub * 128:nk * T + (sub + 1) * 128],
                            wdqt[:, nk * 1024 + oh * 512:nk * 1024 + (oh + 1) * 512],
                            start=(nk == 0), stop=(nk == 21)),
                           waits=[("VE", vprev), ("SY", max(sld, swd))], inc=(1 if nk == 21 else 0))
                    pd = ctx.val("PE")
                    V(lambda e, sub=sub, oh=oh, ps=ps: e.scalar_tensor_tensor(
                        outb[:, oh * T:(oh + 1) * T], ps[:], 0.0,
                        x1r2[:, sub * 1024 + oh * 512:sub * 1024 + (oh + 1) * 512],
                        OP.bypass, OP.add),
                      waits=[("PE", pd), ("SY", ctx.val("SY"))], inc=1)
                    vstt[oh] = ctx.val("VE")
                    vo = ctx.val("VE")
                    Sy(lambda e, kd=kd, sub=sub, oh=oh: e.dma_start(
                        out=out_d.rearrange("(a p) d -> p a d", p=128)[:, kd * 4 + sub, oh * 512:(oh + 1) * 512],
                        in_=outb[:, oh * T:(oh + 1) * T]), waits=[("VE", vo)], inc=16)
        final_sy = ctx.val("SY")
        Sy(lambda e: e.wait_ge(sems["SY"], final_sy), inc=0)

    ctx0 = Ctx()
    walk(ctx0)

    with nc.Block() as block:
        def emit(eng_name):
            def inner(e):
                ctx = Ctx(sems, eng_name, e)
                walk(ctx)
            return inner
        block.sync(emit("SY"))
        block.gpsimd(emit("GP"))
        block.vector(emit("VE"))
        block.scalar(emit("AC"))
        block.tensor(emit("PE"))
    return nc


_NC_CACHE = None


def kernel(**inputs):
    global _NC_CACHE
    u = np.ascontiguousarray(np.asarray(inputs["L0_attn_u"], dtype=np.float32))
    base = {
        "w1": np.ascontiguousarray(np.asarray(inputs["L0_attn_in_proj_w"], dtype=np.float32)),
        "cw": np.ascontiguousarray(np.asarray(inputs["L0_attn_conv_w"], dtype=np.float32)),
        "cb": np.ascontiguousarray(np.asarray(inputs["L0_attn_conv_b"], dtype=np.float32)),
        "w2": np.ascontiguousarray(np.asarray(inputs["L0_attn_out_proj_w"], dtype=np.float32)),
        "lng": np.ascontiguousarray(np.asarray(inputs["L0_attn_ln_g"], dtype=np.float32)),
        "lnb": np.ascontiguousarray(np.asarray(inputs["L0_attn_ln_b"], dtype=np.float32)),
        "gng": np.ascontiguousarray(np.asarray(inputs["L0_attn_gn_g"], dtype=np.float32)),
        "gnb": np.ascontiguousarray(np.asarray(inputs["L0_attn_gn_b"], dtype=np.float32)),
        "alog": np.ascontiguousarray(np.asarray(inputs["L0_attn_A_log"], dtype=np.float32)),
        "dvec": np.ascontiguousarray(np.asarray(inputs["L0_attn_D"], dtype=np.float32)),
        "wg": np.ascontiguousarray(np.asarray(inputs["L0_ffn_w_gate"], dtype=np.float32)),
        "wu": np.ascontiguousarray(np.asarray(inputs["L0_ffn_w_up"], dtype=np.float32)),
        "wd": np.ascontiguousarray(np.asarray(inputs["L0_ffn_w_down"], dtype=np.float32)),
    }
    if _NC_CACHE is None:
        _NC_CACHE = build_program()
    nc = _NC_CACHE
    in_maps = [dict(base, u=np.ascontiguousarray(u[i])) for i in range(8)]
    res = run_bass_kernel_spmd(nc, in_maps, core_ids=list(range(8)))
    out = np.stack([np.asarray(res.results[i]["out"]) for i in range(8)], axis=0)
    return out.astype(np.float32)


if __name__ == "__main__":
    build_program()
    print("built ok")
